# version 11
# baseline (speedup 1.0000x reference)
import numpy as np
import ml_dtypes

import concourse.bass as bass
import concourse.mybir as mybir
from concourse.bass_utils import run_bass_kernel_spmd

F32 = mybir.dt.float32
BF16 = mybir.dt.bfloat16
I32 = mybir.dt.int32
Alu = mybir.AluOpType
Act = mybir.ActivationFunctionType
AxX = mybir.AxisListType.X
BB = ml_dtypes.bfloat16

B, S, DM, H, D = 4, 4096, 1024, 16, 64
OWN = 2048
NKG = 32
NQT = OWN // 128
RING = 8
GRP = 4


def build_program():
    nc = bass.Bass("TRN2", target_bir_lowering=False)

    xb = nc.dram_tensor("xb", [S, DM], F32, kind="ExternalInput").ap()
    qkv_w = nc.dram_tensor("qkv_w", [3 * DM, DM], F32, kind="ExternalInput").ap()
    qkv_b = nc.dram_tensor("qkv_b", [3 * DM], F32, kind="ExternalInput").ap()
    proj_w = nc.dram_tensor("proj_w", [DM, DM], F32, kind="ExternalInput").ap()
    proj_b = nc.dram_tensor("proj_b", [DM], F32, kind="ExternalInput").ap()
    dwc_w = nc.dram_tensor("dwc_w", [DM, 3], F32, kind="ExternalInput").ap()
    dwc_b = nc.dram_tensor("dwc_b", [DM], F32, kind="ExternalInput").ap()
    ffn_w = nc.dram_tensor("ffn_w", [DM, DM], F32, kind="ExternalInput").ap()
    ffn_b = nc.dram_tensor("ffn_b", [DM], F32, kind="ExternalInput").ap()
    selin = nc.dram_tensor("selin", [64 * 128, 128], BF16, kind="ExternalInput").ap()
    permin = nc.dram_tensor("permin", [128, 128], BF16, kind="ExternalInput").ap()
    identin = nc.dram_tensor("identin", [128, 128], BF16, kind="ExternalInput").ap()
    hmaskin = nc.dram_tensor("hmaskin", [2], F32, kind="ExternalInput").ap()

    out = nc.dram_tensor("out", [OWN, DM], F32, kind="ExternalOutput").ap()

    qkvs = nc.dram_tensor("qkvs", [3 * DM, S], BF16).ap()
    wq2 = nc.dram_tensor("wq2", [DM, 3 * DM], BF16).ap()
    xfs = nc.dram_tensor("xfs", [OWN, DM], BF16).ap()
    qw2 = nc.dram_tensor("qw2", [DM, DM], BF16).ap()
    ksd = nc.dram_tensor("ksd", [S], BF16).ap()
    qsd = nc.dram_tensor("qsd", [OWN], BF16).ap()
    zs = nc.dram_tensor("zs", [OWN, DM], F32).ap()
    sc_d = nc.dram_tensor("sc_d", [8], F32).ap()
    cc_in = nc.dram_tensor("cc_in", [1, 2], F32).ap()
    cc_out = nc.dram_tensor("cc_out", [8, 2], F32).ap()

    import contextlib
    stk = contextlib.ExitStack()
    with stk:
        A = lambda nm, shp, dt: stk.enter_context(nc.sbuf_tensor(nm, shp, dt))
        sel = A("sel", [128, 64 * 128], BF16)
        perm = A("perm", [64, 2 * 128], BF16)
        ident = A("ident", [128, 128], BF16)
        hmask = A("hmask", [64, 2], F32)
        wqTb = A("wqTb", [128, 2 * 1024], BF16)
        wpT = A("wpT", [128, 8 * 1024], BF16)
        wf32 = A("wf32", [128, 1024], F32)
        qwtile = A("qwtile", [128, 1024], BF16)
        biasq = A("biasq", [128, 24], F32)
        biasp = A("biasp", [128, 1024], BF16)
        biasf = A("biasf", [128, 1024], BF16)
        dwcw = A("dwcw", [64, 48], F32)
        dwcb = A("dwcb", [64, 16], F32)
        xtile = A("xtile", [128, DM], F32)
        sq = A("sq", [128, DM], F32)
        stat = A("stat", [128, 12], F32)
        xnt = A("xnt", [128, DM], BF16)
        xnT = A("xnT", [128, 8 * 512], BF16)
        qkvtile = A("qkvtile", [128, 512], BF16)
        kT = A("kT", [64, S], BF16)
        vT = A("vT", [64, S], BF16)
        qTh = A("qTh", [64, OWN], BF16)
        kdup = A("kdup", [128, 2048], F32)
        qdup = A("qdup", [128, 1024], BF16)
        vtok = A("vtok", [128, NKG * 65], BF16)
        cv1 = A("cv1", [64, 1024], F32)
        cv2 = A("cv2", [64, 1024], F32)
        convb = A("convb", [64, OWN], BF16)
        ltok = A("ltok", [128, NQT * 64], BF16)
        tbuf = A("tbuf", [128, RING * 1024], BF16)
        kern = A("kern", [128, 2 * 1024], BF16)
        usb = A("usb", [65, 512], BF16)
        utok = A("utok", [128, 4 * 65], BF16)
        den = A("den", [128, 4], F32)
        vpair = A("vpair", [64, 128], BF16)
        onec = A("onec", [64, 2], BF16)
        kstall = A("kstall", [2, 32 * 128], BF16)
        qs2 = A("qs2", [2, OWN], BF16)
        ytok = A("ytok", [128, NQT * DM], BF16)
        yTt = A("yTt", [128, 8 * 128], BF16)
        ptile = A("ptile", [128, DM], F32)
        ztile = A("ztile", [128, DM], F32)
        xft = A("xft", [128, DM], BF16)
        mcol = A("mcol", [128, 4], F32)
        scal = A("scal", [1, 32], F32)
        scol = A("scol", [128, 4], F32)
        qint = A("qint", [128, DM], BF16)
        qiT = A("qiT", [128, 8 * 128], BF16)
        ftile = A("ftile", [128, DM], F32)

        psD0 = stk.enter_context(nc.psum_tensor("psD0", [128, 1024], F32))
        psD1 = stk.enter_context(nc.psum_tensor("psD1", [128, 1024], F32))
        psU0 = stk.enter_context(nc.psum_tensor("psU0", [128, 512], F32))
        psU1 = stk.enter_context(nc.psum_tensor("psU1", [128, 512], F32))
        psT = stk.enter_context(nc.psum_tensor("psT", [128, 512], BF16))
        psK = stk.enter_context(nc.psum_tensor("psK", [128, 512], F32))
        psD = [psD0, psD1]
        psU = [psU0, psU1]

        sems, cnt = {}, {}
        for s in ["sd", "sv", "ss", "stt", "sg"]:
            sems[s] = stk.enter_context(nc.semaphore(s))
            cnt[s] = 0

        blk = stk.enter_context(nc.Block())

        def I(instr, s, n):
            instr.then_inc(sems[s], n)
            cnt[s] += n
            return cnt[s]

        def W(eng, s, val=None):
            eng.wait_ge(sems[s], cnt[s] if val is None else val)

        SY, V, SC, T, G = nc.sync, nc.vector, nc.scalar, nc.tensor, nc.gpsimd

        def ln_tile(src, dst_bf16):
            W(V, "sv")
            W(SC, "sv")
            I(V.tensor_reduce(out=stat[:, 0:1], in_=src, axis=AxX, op=Alu.add), "sv", 1)
            W(SC, "sv")
            I(SC.activation(out=sq[:], in_=src, func=Act.Square, accum_out=stat[:, 1:2]), "ss", 1)
            W(V, "ss")
            I(V.tensor_scalar(out=stat[:, 2:3], in0=stat[:, 0:1], scalar1=1.0 / DM,
                              scalar2=None, op0=Alu.mult), "sv", 1)
            W(V, "sv")
            I(V.tensor_tensor(out=stat[:, 3:4], in0=stat[:, 2:3], in1=stat[:, 2:3], op=Alu.mult), "sv", 1)
            I(V.tensor_scalar(out=stat[:, 4:5], in0=stat[:, 1:2], scalar1=1.0 / DM,
                              scalar2=None, op0=Alu.mult), "sv", 1)
            W(V, "sv")
            I(V.tensor_tensor(out=stat[:, 5:6], in0=stat[:, 4:5], in1=stat[:, 3:4], op=Alu.subtract), "sv", 1)
            W(SC, "sv")
            I(SC.activation(out=stat[:, 6:7], in_=stat[:, 5:6], func=Act.Sqrt, bias=stat[:, 10:11], scale=1.0), "ss", 1)
            W(V, "ss")
            I(V.reciprocal(out=stat[:, 7:8], in_=stat[:, 6:7]), "sv", 1)
            W(V, "sv")
            I(V.tensor_scalar(out=dst_bf16, in0=src, scalar1=stat[:, 2:3], scalar2=stat[:, 7:8],
                              op0=Alu.subtract, op1=Alu.mult), "sv", 1)

        def transpose_block(src_ap, dst_ap, n_ident, width=128):
            W(T, "sv")
            I(T.transpose(psT[0:width, 0:n_ident], src_ap, ident[0:n_ident, 0:n_ident]), "stt", 1)
            W(V, "stt")
            I(V.tensor_copy(out=dst_ap, in_=psT[0:width, 0:n_ident]), "sv", 1)

        @blk.sync
        def _(_sy):
            I(V.memset(stat[:, 10:11], 1e-5), "sv", 1)
            I(V.memset(onec[:, 0:1], 1.0), "sv", 1)
            I(V.memset(onec[:, 1:2], -1.0), "sv", 1)
            W(SY, "sd")
            I(SY.dma_start(out=ident[:], in_=identin[:]), "sd", 16)
            W(SY, "sd")
            I(SY.dma_start(out=perm[:, 0:128], in_=permin[0:64, :]), "sd", 16)
            W(SY, "sd")
            I(SY.dma_start(out=perm[:, 128:256], in_=permin[64:128, :]), "sd", 16)
            W(SY, "sd")
            I(SY.dma_start(out=hmask[:], in_=bass.AP(hmaskin.tensor, 0, [[0, 64], [1, 2]])), "sd", 16)
            for i in range(64):
                W(SY, "sd")
                I(SY.dma_start(out=sel[:, i * 128:(i + 1) * 128],
                               in_=selin[i * 128:(i + 1) * 128, :]), "sd", 16)
            for oc in range(24):
                W(SY, "sd")
                I(SY.dma_start(out=biasq[:, oc:oc + 1],
                               in_=bass.AP(qkv_b.tensor, oc * 128, [[1, 128], [1, 1]])), "sd", 16)
            W(SY, "sd")
            I(SY.dma_start(out=wf32[:], in_=bass.AP(proj_b.tensor, 0, [[0, 128], [1, 1024]])), "sd", 16)
            W(V, "sd")
            I(V.tensor_copy(out=biasp[:], in_=wf32[:]), "sv", 1)
            W(SY, "sv")
            W(SY, "sd")
            I(SY.dma_start(out=wf32[:], in_=bass.AP(ffn_b.tensor, 0, [[0, 128], [1, 1024]])), "sd", 16)
            W(V, "sd")
            I(V.tensor_copy(out=biasf[:], in_=wf32[:]), "sv", 1)
            for h in range(16):
                W(SY, "sd")
                I(SY.dma_start(out=dwcw[:, h * 3:h * 3 + 3],
                               in_=bass.AP(dwc_w.tensor, h * 192, [[3, 64], [1, 3]])), "sd", 16)
                W(SY, "sd")
                I(SY.dma_start(out=dwcb[:, h:h + 1],
                               in_=bass.AP(dwc_b.tensor, h * 64, [[1, 64], [1, 1]])), "sd", 16)

            for oc in range(24):
                W(SY, "sv")
                W(SY, "sd")
                I(SY.dma_start(out=wf32[:], in_=qkv_w[oc * 128:(oc + 1) * 128, :]), "sd", 16)
                W(V, "sd")
                W(V, "sd")
                I(V.tensor_copy(out=qwtile[:], in_=wf32[:]), "sv", 1)
                for dmc in range(8):
                    transpose_block(qwtile[:, dmc * 128:(dmc + 1) * 128],
                                    wqTb[:, dmc * 128:(dmc + 1) * 128], 128)
                W(SY, "sv")
                W(SY, "sd")
                I(SY.dma_start(out=bass.AP(wq2.tensor, oc * 128,
                                           [[3 * DM, 128], [128 * 3 * DM, 8], [1, 128]]),
                               in_=wqTb[:, 0:1024]), "sd", 16)
            for oc in range(8):
                W(SY, "sv")
                W(SY, "sd")
                I(SY.dma_start(out=wf32[:], in_=proj_w[oc * 128:(oc + 1) * 128, :]), "sd", 16)
                W(V, "sd")
                I(V.tensor_copy(out=qwtile[:], in_=wf32[:]), "sv", 1)
                for dmc in range(8):
                    transpose_block(qwtile[:, dmc * 128:(dmc + 1) * 128],
                                    wpT[:, dmc * 1024 + oc * 128: dmc * 1024 + (oc + 1) * 128], 128)

            wq_loaded = [None, None]
            wq_used = [None, None]
            for ch in range(8):
                for tt in range(4):
                    t0 = ch * 4 + tt
                    W(SY, "sv")
                    W(SY, "sd")
                    I(SY.dma_start(out=xtile[:], in_=xb[t0 * 128:(t0 + 1) * 128, :]), "sd", 16)
                    W(V, "sd")
                    W(SC, "sd")
                    ln_tile(xtile[:], xnt[:])
                    for dmc in range(8):
                        transpose_block(xnt[:, dmc * 128:(dmc + 1) * 128],
                                        xnT[:, dmc * 512 + tt * 128: dmc * 512 + (tt + 1) * 128], 128)
                ocs = list(range(8, 24)) + (list(range(0, 8)) if ch < 4 else [])
                for j_oc, oc in enumerate(ocs):
                    par = j_oc % 2
                    if j_oc >= 2:
                        W(SY, "stt", wq_used[par])
                    W(SY, "sd")
                    I(SY.dma_start(out=wqTb[:, par * 1024:(par + 1) * 1024],
                                   in_=bass.AP(wq2.tensor, oc * 128,
                                               [[3 * DM, 128], [128 * 3 * DM, 8], [1, 128]])), "sd", 16)
                    wq_loaded[par] = cnt["sd"]
                    W(T, "sv")
                    W(T, "ss")
                    W(T, "sd", wq_loaded[par])
                    mm = None
                    for dmc in range(8):
                        mm = T.matmul(out=psK[:, 0:512],
                                      lhsT=wqTb[:, par * 1024 + dmc * 128: par * 1024 + (dmc + 1) * 128],
                                      rhs=xnT[:, dmc * 512:(dmc + 1) * 512],
                                      start=(dmc == 0), stop=(dmc == 7))
                    wq_used[par] = I(mm, "stt", 1)
                    W(V, "stt")
                    W(V, "sd")
                    I(V.tensor_scalar(out=qkvtile[:], in0=psK[:, 0:512], scalar1=biasq[:, oc:oc + 1],
                                      scalar2=None, op0=Alu.add), "sv", 1)
                    W(SY, "sv")
                    W(SY, "sd")
                    I(SY.dma_start(out=qkvs[oc * 128:(oc + 1) * 128, ch * 512:(ch + 1) * 512],
                                   in_=qkvtile[:]), "sd", 16)

            for h in range(H):
                W(SY, "sv")
                W(SY, "stt")
                W(SY, "sd")
                I(SY.dma_start(out=kT[:], in_=qkvs[1024 + h * 64:1024 + (h + 1) * 64, :]), "sd", 16)
                W(SY, "sd")
                I(SY.dma_start(out=vT[:], in_=qkvs[2048 + h * 64:2048 + (h + 1) * 64, :]), "sd", 16)
                W(SY, "sd")
                I(SY.dma_start(out=qTh[:], in_=qkvs[h * 64:(h + 1) * 64, 0:OWN]), "sd", 16)

                W(V, "sd")
                for kg in range(NKG):
                    W(V, "stt")
                    I(V.tensor_copy(out=vpair[:, 0:64], in_=vT[:, kg * 64:(kg + 1) * 64]), "sv", 1)
                    I(V.tensor_copy(out=vpair[:, 64:128], in_=vT[:, 2048 + kg * 64:2048 + (kg + 1) * 64]), "sv", 1)
                    W(T, "sv")
                    W(T, "sd")
                    I(T.transpose(psT[:, 0:64], vpair[:], ident[0:64, 0:64]), "stt", 1)
                    W(V, "stt")
                    I(V.tensor_copy(out=vtok[:, kg * 65:kg * 65 + 64], in_=psT[:, 0:64]), "sv", 1)
                I(V.memset(bass.AP(vtok, 64, [[NKG * 65, 128], [65, NKG]]), 1.0), "sv", 1)

                w0 = dwcw[:, h * 3 + 0:h * 3 + 1]
                w1 = dwcw[:, h * 3 + 1:h * 3 + 2]
                w2 = dwcw[:, h * 3 + 2:h * 3 + 3]
                I(V.tensor_tensor(out=stat[0:64, 8:9], in0=w0, in1=hmask[:, 0:1], op=Alu.mult), "sv", 1)
                I(V.tensor_tensor(out=stat[0:64, 9:10], in0=w2, in1=hmask[:, 1:2], op=Alu.mult), "sv", 1)
                for hf in range(2):
                    c0 = hf * 1024
                    W(V, "sv")
                    I(V.tensor_scalar(out=cv1[:], in0=vT[:, c0:c0 + 1024], scalar1=w1, scalar2=None,
                                      op0=Alu.mult), "sv", 1)
                    W(V, "sv")
                    I(V.scalar_tensor_tensor(out=cv2[:, 1:1024], in0=vT[:, c0:c0 + 1023], scalar=w0,
                                             in1=cv1[:, 1:1024], op0=Alu.mult, op1=Alu.add), "sv", 1)
                    if hf == 0:
                        I(V.scalar_tensor_tensor(out=cv2[:, 0:1], in0=vT[:, S - 1:S],
                                                 scalar=stat[0:64, 8:9],
                                                 in1=cv1[:, 0:1], op0=Alu.mult, op1=Alu.add), "sv", 1)
                    else:
                        I(V.scalar_tensor_tensor(out=cv2[:, 0:1], in0=vT[:, c0 - 1:c0], scalar=w0,
                                                 in1=cv1[:, 0:1], op0=Alu.mult, op1=Alu.add), "sv", 1)
                    W(V, "sv")
                    I(V.scalar_tensor_tensor(out=cv1[:, 0:1023], in0=vT[:, c0 + 1:c0 + 1024], scalar=w2,
                                             in1=cv2[:, 0:1023], op0=Alu.mult, op1=Alu.add), "sv", 1)
                    if hf == 1:
                        I(V.scalar_tensor_tensor(out=cv1[:, 1023:1024], in0=vT[:, OWN:OWN + 1],
                                                 scalar=stat[0:64, 9:10],
                                                 in1=cv2[:, 1023:1024], op0=Alu.mult, op1=Alu.add), "sv", 1)
                    else:
                        I(V.scalar_tensor_tensor(out=cv1[:, 1023:1024], in0=vT[:, c0 + 1024:c0 + 1025],
                                                 scalar=w2,
                                                 in1=cv2[:, 1023:1024], op0=Alu.mult, op1=Alu.add), "sv", 1)
                    W(V, "sv")
                    I(V.tensor_scalar(out=convb[:, c0:c0 + 1024], in0=cv1[:], scalar1=dwcb[:, h:h + 1],
                                      scalar2=None, op0=Alu.add), "sv", 1)
                for j in range(NQT):
                    transpose_block(convb[:, j * 128:(j + 1) * 128], ltok[:, j * 64:(j + 1) * 64],
                                    64, width=128)

                for chk in range(8):
                    W(T, "sv"); W(T, "ss")
                    mm = T.matmul(out=psK[0:1, 0:512], lhsT=onec[:, 1:2],
                                  rhs=kT[:, chk * 512:(chk + 1) * 512], start=True, stop=True)
                    I(mm, "stt", 1)
                    W(SC, "stt")
                    W(SC, "sd")
                    I(SC.activation(out=usb[0:1, 0:512], in_=psK[0:1, 0:512], func=Act.Copy), "ss", 1)
                    W(SY, "ss")
                    W(SY, "sd")
                    I(SY.dma_start(out=ksd[chk * 512:(chk + 1) * 512], in_=usb[0:1, 0:512]), "sd", 16)
                for chk in range(4):
                    W(T, "sv"); W(T, "ss")
                    mm = T.matmul(out=psK[0:1, 0:512], lhsT=onec[:, 0:1],
                                  rhs=qTh[:, chk * 512:(chk + 1) * 512], start=True, stop=True)
                    I(mm, "stt", 1)
                    W(SC, "stt")
                    W(SC, "sd")
                    I(SC.activation(out=usb[0:1, 0:512], in_=psK[0:1, 0:512], func=Act.Copy), "ss", 1)
                    W(SY, "ss")
                    W(SY, "sd")
                    I(SY.dma_start(out=qsd[chk * 512:(chk + 1) * 512], in_=usb[0:1, 0:512]), "sd", 16)
                W(V, "stt")
                I(V.memset(qs2[0:2, :], 1.0), "sv", 1)
                I(V.memset(kstall[0:2, :], 1.0), "sv", 1)
                W(SY, "sv"); W(SY, "sd")
                W(SY, "sd")
                I(SY.dma_start(out=qs2[1:2, :], in_=bass.AP(qsd.tensor, 0, [[0, 1], [1, OWN]])), "sd", 16)
                W(SY, "stt")
                for kg2 in range(NKG):
                    W(SY, "sd")
                    I(SY.dma_start(out=kstall[0:1, kg2 * 128:(kg2 + 1) * 128],
                                   in_=bass.AP(ksd.tensor, kg2 * 64,
                                               [[0, 1], [2048, 2], [1, 64]])), "sd", 16)
                kst_ready = cnt["sd"]

                for kg in range(NKG):
                    W(T, "sv")
                    W(T, "ss")
                    T.matmul(out=psK[:, 0:64], lhsT=perm[:, 0:128],
                             rhs=kT[:, kg * 64:(kg + 1) * 64], start=True, stop=False)
                    mm = T.matmul(out=psK[:, 0:64], lhsT=perm[:, 128:256],
                                  rhs=kT[:, 2048 + kg * 64:2048 + (kg + 1) * 64],
                                  start=False, stop=True)
                    I(mm, "stt", 1)
                    W(V, "stt")
                    I(V.tensor_copy(out=kdup[:, kg * 64:(kg + 1) * 64], in_=psK[:, 0:64]), "sv", 1)

                for qh2 in range(2):
                    for qc in range(2):
                        W(T, "sv")
                        W(T, "ss")
                        T.matmul(out=psK[:, 0:512], lhsT=perm[:, 0:128],
                                 rhs=qTh[:, qh2 * 1024 + qc * 512: qh2 * 1024 + (qc + 1) * 512],
                                 start=True, stop=False)
                        mm = T.matmul(out=psK[:, 0:512], lhsT=perm[:, 128:256],
                                      rhs=qTh[:, qh2 * 1024 + qc * 512: qh2 * 1024 + (qc + 1) * 512],
                                      start=False, stop=True)
                        I(mm, "stt", 1)
                        W(V, "stt")
                        I(V.tensor_copy(out=qdup[:, qc * 512:(qc + 1) * 512], in_=psK[:, 0:512]), "sv", 1)

                    slot_free = [None] * RING
                    prod_cnt = [None] * RING
                    prod_sem = [None] * RING
                    exp_hist = {}
                    umm_hist = {}
                    W(V, "sv")
                    W(SC, "sv")
                    W(SC, "stt")
                    for kg in range(NKG):
                        is_abs = ((kg + 1) * 11) // 32 > (kg * 11) // 32
                        for i in range(64):
                            gidx = kg * 64 + i
                            slot = gidx % RING
                            if gidx % GRP == 0:
                                tgt = slot_free[(gidx + GRP - 1) % RING]
                                if tgt is not None:
                                    W(V if not is_abs else SC, "stt", tgt)
                            if is_abs:
                                pr = SC.activation(out=tbuf[:, slot * 1024:(slot + 1) * 1024],
                                                   in_=qdup[:], func=Act.Abs, scale=-1.0,
                                                   bias=kdup[:, gidx:gidx + 1])
                                prod_cnt[slot] = I(pr, "ss", 1)
                                prod_sem[slot] = "ss"
                            else:
                                ts = V.tensor_scalar(out=tbuf[:, slot * 1024:(slot + 1) * 1024],
                                                     in0=qdup[:], scalar1=kdup[:, gidx:gidx + 1],
                                                     scalar2=0.0, op0=Alu.subtract, op1=Alu.min)
                                prod_cnt[slot] = I(ts, "sv", 1)
                                prod_sem[slot] = "sv"
                            if gidx % GRP == 0:
                                W(T, prod_sem[slot], prod_cnt[slot] + GRP - 1)
                            if i == 0:
                                if kg >= 2:
                                    W(T, "ss", exp_hist[kg - 2])
                                elif kg < 2:
                                    W(T, "ss")
                            mm = None
                            for qc in range(2):
                                mm = T.matmul(out=psD[kg % 2][:, qc * 512:(qc + 1) * 512],
                                              lhsT=sel[:, i * 128:(i + 1) * 128],
                                              rhs=tbuf[:, slot * 1024 + qc * 512: slot * 1024 + (qc + 1) * 512],
                                              start=(i == 0), stop=(is_abs and i == 63))
                            slot_free[slot] = I(mm, "stt", 1)
                        if not is_abs:
                            if kg == 0 and qh2 == 0:
                                W(T, "sd", kst_ready)
                            mm = None
                            for qc in range(2):
                                mm = T.matmul(out=psD[kg % 2][:, qc * 512:(qc + 1) * 512],
                                              lhsT=kstall[:, kg * 128:(kg + 1) * 128],
                                              rhs=qs2[:, qh2 * 1024 + qc * 512: qh2 * 1024 + (qc + 1) * 512],
                                              start=False, stop=True)
                            I(mm, "stt", 1)
                        W(SC, "stt")
                        if kg >= 2:
                            W(SC, "stt", umm_hist[kg - 2])
                        a = SC.activation(out=kern[:, (kg % 2) * 1024:(kg % 2 + 1) * 1024],
                                          in_=psD[kg % 2][:], func=Act.Exp,
                                          scale=(0.03125 if is_abs else -0.0625))
                        exp_hist[kg] = I(a, "ss", 1)
                        if kg >= 1:
                            pk = (kg - 1) % 2
                            if kg == 1:
                                W(T, "ss")
                            else:
                                W(T, "ss", exp_hist[kg - 1])
                            mm = None
                            for qc in range(2):
                                mm = T.matmul(out=psU[qc][0:65, :],
                                              lhsT=vtok[:, (kg - 1) * 65:(kg - 1) * 65 + 65],
                                              rhs=kern[:, pk * 1024 + qc * 512: pk * 1024 + (qc + 1) * 512],
                                              start=(kg - 1 == 0), stop=False)
                            umm_hist[kg - 1] = I(mm, "stt", 1)
                    W(T, "ss", exp_hist[31])
                    mm = None
                    for qc in range(2):
                        mm = T.matmul(out=psU[qc][0:65, :],
                                      lhsT=vtok[:, 31 * 65:31 * 65 + 65],
                                      rhs=kern[:, (31 % 2) * 1024 + qc * 512: (31 % 2) * 1024 + (qc + 1) * 512],
                                      start=False, stop=True)
                    umm_hist[31] = I(mm, "stt", 1)

                    for qc in range(2):
                        W(SC, "stt")
                        I(SC.activation(out=usb[:], in_=psU[qc][0:65, :], func=Act.Copy), "ss", 1)
                        for j in range(4):
                            jg = qh2 * 8 + qc * 4 + j
                            W(T, "ss")
                            W(T, "sv")
                            I(T.transpose(psT[:, 0:65], usb[:, j * 128:(j + 1) * 128],
                                          ident[0:65, 0:65]), "stt", 1)
                            W(V, "stt")
                            I(V.tensor_copy(out=utok[:, j * 65:(j + 1) * 65],
                                            in_=psT[:, 0:65]), "sv", 1)
                            W(V, "sv")
                            I(V.tensor_scalar(out=den[:, j:j + 1],
                                              in0=utok[:, j * 65 + 64:j * 65 + 65],
                                              scalar1=1e-6, scalar2=None, op0=Alu.add), "sv", 1)
                            W(V, "sv")
                            I(V.reciprocal(out=den[:, j:j + 1], in_=den[:, j:j + 1]), "sv", 1)
                            W(V, "sv")
                            I(V.scalar_tensor_tensor(
                                out=ytok[:, jg * DM + h * 64: jg * DM + (h + 1) * 64],
                                in0=utok[:, j * 65:j * 65 + 64], scalar=den[:, j:j + 1],
                                in1=ltok[:, jg * 64:(jg + 1) * 64],
                                op0=Alu.mult, op1=Alu.add), "sv", 1)

            I(V.memset(mcol[:, 0:1], 0.0), "sv", 1)
            for tt in range(NQT):
                for cc in range(8):
                    transpose_block(ytok[:, tt * DM + cc * 128: tt * DM + (cc + 1) * 128],
                                    yTt[:, cc * 128:(cc + 1) * 128], 128)
                for o2c in range(2):
                    W(T, "sv")
                    mm = None
                    for cc in range(8):
                        mm = T.matmul(out=psD[o2c][:, 0:512],
                                      lhsT=yTt[:, cc * 128:(cc + 1) * 128],
                                      rhs=wpT[:, cc * 1024 + o2c * 512: cc * 1024 + (o2c + 1) * 512],
                                      start=(cc == 0), stop=(cc == 7))
                    I(mm, "stt", 1)
                W(V, "stt")
                for o2c in range(2):
                    I(V.tensor_tensor(out=ptile[:, o2c * 512:(o2c + 1) * 512],
                                      in0=psD[o2c][:, 0:512],
                                      in1=biasp[:, o2c * 512:(o2c + 1) * 512], op=Alu.add), "sv", 1)
                ln_tile(ptile[:], xnt[:])
                W(SY, "sv")
                W(SY, "sd")
                I(SY.dma_start(out=xtile[:], in_=xb[tt * 128:(tt + 1) * 128, :]), "sd", 16)
                W(V, "sd")
                I(V.tensor_tensor(out=ztile[:], in0=xtile[:], in1=xnt[:], op=Alu.add), "sv", 1)
                W(SY, "sv")
                W(SY, "sd")
                I(SY.dma_start(out=zs[tt * 128:(tt + 1) * 128, :], in_=ztile[:]), "sd", 16)
                ln_tile(ztile[:], xft[:])
                W(V, "sv")
                I(V.tensor_reduce(out=mcol[:, 1:2], in_=xft[:], axis=AxX,
                                  op=Alu.max, apply_absolute_value=True), "sv", 1)
                W(V, "sv")
                I(V.tensor_tensor(out=mcol[:, 0:1], in0=mcol[:, 0:1], in1=mcol[:, 1:2],
                                  op=Alu.max), "sv", 1)
                W(SY, "sv")
                W(SY, "sd")
                I(SY.dma_start(out=xfs[tt * 128:(tt + 1) * 128, :], in_=xft[:]), "sd", 16)

            I(V.memset(mcol[:, 2:3], 0.0), "sv", 1)
            for oc in range(8):
                W(SY, "sv")
                W(SY, "sd")
                I(SY.dma_start(out=wf32[:], in_=ffn_w[oc * 128:(oc + 1) * 128, :]), "sd", 16)
                W(V, "sd")
                I(V.tensor_reduce(out=mcol[:, 3:4], in_=wf32[:], axis=AxX, op=Alu.add,
                                  apply_absolute_value=True), "sv", 1)
                W(V, "sv")
                I(V.tensor_tensor(out=mcol[:, 2:3], in0=mcol[:, 2:3], in1=mcol[:, 3:4],
                                  op=Alu.add), "sv", 1)
            W(V, "sv")
            I(V.tensor_copy(out=qwtile[:, 0:1], in_=mcol[:, 0:1]), "sv", 1)
            I(V.tensor_copy(out=qwtile[:, 1:2], in_=mcol[:, 2:3]), "sv", 1)
            transpose_block(qwtile[:, 0:1], sq[0:1, 0:128], 128, width=1)
            transpose_block(qwtile[:, 1:2], sq[0:1, 128:256], 128, width=1)
            W(V, "sv")
            I(V.tensor_reduce(out=scal[0:1, 0:1], in_=sq[0:1, 0:128], axis=AxX, op=Alu.max), "sv", 1)
            I(V.tensor_reduce(out=scal[0:1, 1:2], in_=sq[0:1, 128:256], axis=AxX, op=Alu.add), "sv", 1)
            W(V, "sv")
            I(V.tensor_scalar(out=scal[0:1, 1:2], in0=scal[0:1, 1:2], scalar1=float(2.0 ** -20),
                              scalar2=None, op0=Alu.mult), "sv", 1)
            W(V, "sv")
            I(V.tensor_scalar(out=scal[0:1, 2:3], in0=scal[0:1, 0:1], scalar1=float(2.0 ** 18),
                              scalar2=None, op0=Alu.mult), "sv", 1)
            W(V, "sv")
            I(V.tensor_copy(out=ftile[0:1, 0:1].bitcast(I32), in_=scal[0:1, 2:3]), "sv", 1)
            W(V, "sv")
            I(V.tensor_copy(out=scal[0:1, 2:3], in_=ftile[0:1, 0:1].bitcast(I32)), "sv", 1)
            W(V, "sv")
            I(V.scalar_tensor_tensor(out=scal[0:1, 3:4], in0=scal[0:1, 2:3],
                                     scalar=float(-(2.0 ** -18)), in1=scal[0:1, 0:1],
                                     op0=Alu.mult, op1=Alu.add), "sv", 1)
            W(V, "sv")
            I(V.tensor_scalar(out=scal[0:1, 4:5], in0=scal[0:1, 3:4], scalar1=float(2.0 ** 42),
                              scalar2=None, op0=Alu.mult), "sv", 1)
            W(V, "sv")
            I(V.tensor_copy(out=ftile[0:1, 1:2].bitcast(I32), in_=scal[0:1, 4:5]), "sv", 1)
            W(V, "sv")
            I(V.tensor_copy(out=scal[0:1, 4:5], in_=ftile[0:1, 1:2].bitcast(I32)), "sv", 1)
            W(V, "sv")
            I(V.tensor_copy(out=scal[0:1, 5:6], in_=scal[0:1, 2:3]), "sv", 1)
            I(V.tensor_copy(out=scal[0:1, 6:7], in_=scal[0:1, 4:5]), "sv", 1)
            W(SC, "sv")
            W(SC, "ss")
            I(SC.dma_start(out=cc_in[:], in_=scal[0:1, 5:7]), "ss", 16)
            W(G, "ss")
            I(G.collective_compute("AllGather", Alu.bypass, ins=[cc_in[:]], outs=[cc_out[:]],
                                   replica_groups=[[0, 1, 2, 3, 4, 5, 6, 7]]), "sg", 1)
            W(SC, "sg")
            W(SC, "ss")
            I(SC.dma_start(out=scal[0:1, 8:24],
                           in_=bass.AP(cc_out.tensor, 0, [[16, 1], [1, 16]])), "ss", 16)
            W(V, "ss")
            h_view = bass.AP(scal, 8, [[32, 1], [2, 8]])
            l_view = bass.AP(scal, 9, [[32, 1], [2, 8]])
            I(V.tensor_scalar(out=scal[0:1, 24:32], in0=h_view, scalar1=float(2.0 ** -18),
                              scalar2=None, op0=Alu.mult), "sv", 1)
            W(V, "sv")
            I(V.scalar_tensor_tensor(out=scal[0:1, 8:16], in0=l_view, scalar=float(2.0 ** -42),
                                     in1=scal[0:1, 24:32], op0=Alu.mult, op1=Alu.add), "sv", 1)
            W(V, "sv")
            I(V.tensor_reduce(out=scal[0:1, 16:17], in_=scal[0:1, 8:16], axis=AxX, op=Alu.max), "sv", 1)
            W(V, "sv")
            I(V.reciprocal(out=scal[0:1, 17:18], in_=scal[0:1, 16:17]), "sv", 1)
            W(V, "sv")
            I(V.tensor_scalar(out=scal[0:1, 18:19], in0=scal[0:1, 17:18], scalar1=127.0,
                              scalar2=None, op0=Alu.mult), "sv", 1)
            I(V.tensor_scalar(out=scal[0:1, 19:20], in0=scal[0:1, 16:17],
                              scalar1=float(1.0 / 127.0), scalar2=None, op0=Alu.mult), "sv", 1)
            W(V, "sv")
            I(V.tensor_tensor(out=scal[0:1, 20:21], in0=scal[0:1, 19:20], in1=scal[0:1, 1:2],
                              op=Alu.mult), "sv", 1)
            I(V.tensor_scalar(out=scal[0:1, 21:22], in0=scal[0:1, 1:2], scalar1=0.5,
                              scalar2=None, op0=Alu.mult), "sv", 1)
            W(SY, "sv")
            W(SY, "sd")
            I(SY.dma_start(out=sc_d[0:4], in_=scal[0:1, 18:22]), "sd", 16)
            W(SY, "sd")
            W(SY, "sd")
            I(SY.dma_start(out=scol[:, 0:4],
                           in_=bass.AP(sc_d.tensor, 0, [[0, 128], [1, 4]])), "sd", 16)

            for oc in range(8):
                W(SY, "sv")
                W(SY, "ss")
                W(SY, "sd")
                I(SY.dma_start(out=wf32[:], in_=ffn_w[oc * 128:(oc + 1) * 128, :]), "sd", 16)
                W(SC, "sd")
                I(SC.activation(out=ptile[:], in_=wf32[:], func=Act.Abs), "ss", 1)
                W(SC, "ss")
                I(SC.activation(out=ztile[:], in_=wf32[:], func=Act.Sign), "ss", 1)
                W(V, "ss")
                W(V, "sv")
                I(V.tensor_scalar(out=ftile[:], in0=ptile[:], scalar1=scol[:, 3:4], scalar2=None,
                                  op0=Alu.is_gt), "sv", 1)
                W(V, "sv")
                W(V, "ss")
                I(V.tensor_tensor(out=qwtile[:], in0=ftile[:], in1=ztile[:], op=Alu.mult), "sv", 1)
                for kc in range(8):
                    transpose_block(qwtile[:, kc * 128:(kc + 1) * 128],
                                    yTt[:, kc * 128:(kc + 1) * 128], 128)
                W(SY, "sv")
                W(SY, "sd")
                I(SY.dma_start(out=bass.AP(qw2.tensor, oc * 128,
                                           [[DM, 128], [128 * DM, 8], [1, 128]]),
                               in_=yTt[:, 0:1024]), "sd", 16)

            qw_loaded = [None] * 4
            qw_used = [None] * 4
            for tt in range(NQT):
                W(SY, "sv")
                W(SY, "sd")
                I(SY.dma_start(out=xft[:], in_=xfs[tt * 128:(tt + 1) * 128, :]), "sd", 16)
                W(V, "stt")
                W(V, "sd")
                I(V.tensor_scalar(out=ftile[:], in0=xft[:],
                                  scalar1=scol[:, 0:1], scalar2=None, op0=Alu.mult), "sv", 1)
                W(SC, "sv")
                I(SC.activation(out=sq[:], in_=ftile[:], func=Act.Sign), "ss", 1)
                W(V, "ss")
                I(V.scalar_tensor_tensor(out=ptile[:], in0=sq[:], scalar=0.5, in1=ftile[:],
                                         op0=Alu.mult, op1=Alu.add), "sv", 1)
                W(V, "sv")
                I(V.tensor_copy(out=ftile[:].bitcast(I32), in_=ptile[:]), "sv", 1)
                W(V, "sv")
                I(V.tensor_copy(out=qint[:], in_=ftile[:].bitcast(I32)), "sv", 1)
                for kc in range(8):
                    transpose_block(qint[:, kc * 128:(kc + 1) * 128],
                                    qiT[:, kc * 128:(kc + 1) * 128], 128)
                for oc2 in range(2):
                    for kc in range(8):
                        slot = kc % 4
                        if qw_used[slot] is not None:
                            W(SY, "stt", qw_used[slot])
                        W(SY, "sd")
                        I(SY.dma_start(out=wqTb[:, slot * 512:(slot + 1) * 512],
                                       in_=bass.AP(qw2.tensor, kc * 128 * DM + oc2 * 512,
                                                   [[DM, 128], [1, 512]])), "sd", 16)
                        qw_loaded[slot] = cnt["sd"]
                        if kc == 0:
                            W(T, "sv")
                        W(T, "sd", qw_loaded[slot])
                        mm = T.matmul(out=psD[oc2][:, 0:512],
                                      lhsT=qiT[:, kc * 128:(kc + 1) * 128],
                                      rhs=wqTb[:, slot * 512:(slot + 1) * 512],
                                      start=(kc == 0), stop=(kc == 7))
                        qw_used[slot] = I(mm, "stt", 1)
                W(V, "stt")
                for oc2 in range(2):
                    I(V.scalar_tensor_tensor(out=ftile[:, oc2 * 512:(oc2 + 1) * 512],
                                             in0=psD[oc2][:, 0:512], scalar=scol[:, 2:3],
                                             in1=biasf[:, oc2 * 512:(oc2 + 1) * 512],
                                             op0=Alu.mult, op1=Alu.add), "sv", 1)
                ln_tile(ftile[:], xnt[:])
                W(SY, "sv")
                W(SY, "sd")
                I(SY.dma_start(out=ztile[:], in_=zs[tt * 128:(tt + 1) * 128, :]), "sd", 16)
                W(V, "sd")
                I(V.tensor_copy(out=ptile[:], in_=xnt[:]), "sv", 1)
                W(V, "sv")
                I(V.tensor_tensor(out=ftile[:], in0=ztile[:], in1=ptile[:], op=Alu.add), "sv", 1)
                W(SY, "sv")
                W(SY, "sd")
                I(SY.dma_start(out=out[tt * 128:(tt + 1) * 128, :], in_=ftile[:]), "sd", 16)

            W(SY, "sd")

    nc.finalize()
    return nc


def _host_constants():
    selv = np.zeros((64 * 128, 128), dtype=np.float32)
    for i in range(64):
        selv[i * 128: i * 128 + 64, i] = -2.0
        selv[i * 128 + 64: i * 128 + 128, 64 + i] = -2.0
    permv = np.zeros((128, 128), dtype=np.float32)
    for d in range(64):
        permv[d, d] = 1.0
        permv[64 + d, 64 + d] = 1.0
    identv = np.eye(128, dtype=np.float32)
    return selv.astype(BB), permv.astype(BB), identv.astype(BB)


_NC_CACHE = None


def kernel(L0_attn_x, L0_attn_qkv_w, L0_attn_qkv_b, L0_attn_proj_w, L0_attn_proj_b,
           L0_attn_dwc_w, L0_attn_dwc_b, L0_ffn_weight, L0_ffn_bias):
    global _NC_CACHE
    if _NC_CACHE is None:
        _NC_CACHE = build_program()
    nc = _NC_CACHE

    x = np.asarray(L0_attn_x, dtype=np.float32)
    selv, permv, identv = _host_constants()
    common = dict(
        qkv_w=np.ascontiguousarray(L0_attn_qkv_w, dtype=np.float32),
        qkv_b=np.ascontiguousarray(L0_attn_qkv_b, dtype=np.float32),
        proj_w=np.ascontiguousarray(L0_attn_proj_w, dtype=np.float32),
        proj_b=np.ascontiguousarray(L0_attn_proj_b, dtype=np.float32),
        dwc_w=np.ascontiguousarray(np.asarray(L0_attn_dwc_w, dtype=np.float32).reshape(DM, 3)),
        dwc_b=np.ascontiguousarray(L0_attn_dwc_b, dtype=np.float32),
        ffn_w=np.ascontiguousarray(L0_ffn_weight, dtype=np.float32),
        ffn_b=np.ascontiguousarray(L0_ffn_bias, dtype=np.float32),
        selin=selv, permin=permv, identin=identv,
    )
    in_maps = []
    for c in range(8):
        b, qh = c // 2, c % 2
        m = dict(common)
        m["xb"] = np.ascontiguousarray(np.roll(x[b], -qh * OWN, axis=0))
        m["hmaskin"] = np.array([1.0 if qh == 1 else 0.0, 1.0 if qh == 0 else 0.0],
                                dtype=np.float32)
        in_maps.append(m)

    res = run_bass_kernel_spmd(nc, in_maps, core_ids=list(range(8)))
    outp = np.empty((B, S, DM), dtype=np.float32)
    for c in range(8):
        b, qh = c // 2, c % 2
        outp[b, qh * OWN:(qh + 1) * OWN] = res.results[c]["out"]
    return outp


if __name__ == "__main__":
    import time
    t0 = time.time()
    prog = build_program()
    n = sum(1 for _ in prog.all_instructions())
    print(f"built in {time.time()-t0:.1f}s, instructions: {n}")


# version 16
# speedup vs baseline: 1.0527x; 1.0527x over previous
import numpy as np
import ml_dtypes

import concourse.bass as bass
import concourse.mybir as mybir
from concourse.bass_utils import run_bass_kernel_spmd

F32 = mybir.dt.float32
BF16 = mybir.dt.bfloat16
I32 = mybir.dt.int32
Alu = mybir.AluOpType
Act = mybir.ActivationFunctionType
AxX = mybir.AxisListType.X
BB = ml_dtypes.bfloat16

B, S, DM, H, D = 4, 4096, 1024, 16, 64
OWN = 2048
NKG = 32
NQT = OWN // 128
RING = 8
GRP = 4


def build_program():
    nc = bass.Bass("TRN2", target_bir_lowering=False)

    xb = nc.dram_tensor("xb", [S, DM], F32, kind="ExternalInput").ap()
    qkv_w = nc.dram_tensor("qkv_w", [3 * DM, DM], F32, kind="ExternalInput").ap()
    qkv_b = nc.dram_tensor("qkv_b", [3 * DM], F32, kind="ExternalInput").ap()
    proj_w = nc.dram_tensor("proj_w", [DM, DM], F32, kind="ExternalInput").ap()
    proj_b = nc.dram_tensor("proj_b", [DM], F32, kind="ExternalInput").ap()
    dwc_w = nc.dram_tensor("dwc_w", [DM, 3], F32, kind="ExternalInput").ap()
    dwc_b = nc.dram_tensor("dwc_b", [DM], F32, kind="ExternalInput").ap()
    ffn_w = nc.dram_tensor("ffn_w", [DM, DM], F32, kind="ExternalInput").ap()
    ffn_b = nc.dram_tensor("ffn_b", [DM], F32, kind="ExternalInput").ap()
    selin = nc.dram_tensor("selin", [64 * 128, 128], BF16, kind="ExternalInput").ap()
    permin = nc.dram_tensor("permin", [128, 128], BF16, kind="ExternalInput").ap()
    identin = nc.dram_tensor("identin", [128, 128], BF16, kind="ExternalInput").ap()
    hmaskin = nc.dram_tensor("hmaskin", [2], F32, kind="ExternalInput").ap()

    out = nc.dram_tensor("out", [OWN, DM], F32, kind="ExternalOutput").ap()

    qkvs = nc.dram_tensor("qkvs", [3 * DM, S], BF16).ap()
    wq2 = nc.dram_tensor("wq2", [DM, 3 * DM], BF16).ap()
    xfs = nc.dram_tensor("xfs", [OWN, DM], BF16).ap()
    qw2 = nc.dram_tensor("qw2", [DM, DM], BF16).ap()
    ksd = nc.dram_tensor("ksd", [S], BF16).ap()
    qsd = nc.dram_tensor("qsd", [OWN], BF16).ap()
    zs = nc.dram_tensor("zs", [OWN, DM], F32).ap()
    sc_d = nc.dram_tensor("sc_d", [8], F32).ap()
    cc_in = nc.dram_tensor("cc_in", [1, 2], F32).ap()
    cc_out = nc.dram_tensor("cc_out", [8, 2], F32).ap()

    import contextlib
    stk = contextlib.ExitStack()
    with stk:
        A = lambda nm, shp, dt: stk.enter_context(nc.sbuf_tensor(nm, shp, dt))
        sel = A("sel", [128, 64 * 128], BF16)
        perm = A("perm", [64, 2 * 128], BF16)
        ident = A("ident", [128, 128], BF16)
        hmask = A("hmask", [64, 2], F32)
        wqTb = A("wqTb", [128, 2 * 1024], BF16)
        wpT = A("wpT", [128, 8 * 1024], BF16)
        wf32 = A("wf32", [128, 1024], F32)
        qwtile = A("qwtile", [128, 1024], BF16)
        biasq = A("biasq", [128, 24], F32)
        biasp = A("biasp", [128, 1024], BF16)
        biasf = A("biasf", [128, 1024], BF16)
        dwcw = A("dwcw", [64, 48], F32)
        dwcb = A("dwcb", [64, 16], F32)
        xtile = A("xtile", [128, DM], F32)
        sq = A("sq", [128, DM], F32)
        stat = A("stat", [128, 12], F32)
        xnt = A("xnt", [128, DM], BF16)
        xnT = A("xnT", [128, 8 * 512], BF16)
        qkvtile = A("qkvtile", [128, 512], BF16)
        kT = A("kT", [64, S], BF16)
        vT = A("vT", [64, S], BF16)
        qTh = A("qTh", [64, OWN], BF16)
        kdup = A("kdup", [128, 2048], F32)
        qdup = A("qdup", [128, 1024], BF16)
        vtok = A("vtok", [128, NKG * 65], BF16)
        cv1 = A("cv1", [64, 1024], F32)
        cv2 = A("cv2", [64, 1024], F32)
        convb = A("convb", [64, OWN], BF16)
        ltok = A("ltok", [128, NQT * 64], BF16)
        tbuf = A("tbuf", [128, RING * 1024], BF16)
        kern = A("kern", [128, 2 * 1024], BF16)
        usb = A("usb", [65, 512], BF16)
        utok = A("utok", [128, 4 * 65], BF16)
        den = A("den", [128, 4], F32)
        vpair = A("vpair", [64, 128], BF16)
        onec = A("onec", [64, 2], BF16)
        kstall = A("kstall", [2, 32 * 128], BF16)
        qs2 = A("qs2", [2, OWN], BF16)
        ytok = A("ytok", [128, NQT * DM], BF16)
        yTt = A("yTt", [128, 8 * 128], BF16)
        ptile = A("ptile", [128, DM], F32)
        ztile = A("ztile", [128, DM], F32)
        xft = A("xft", [128, DM], BF16)
        mcol = A("mcol", [128, 4], F32)
        scal = A("scal", [1, 32], F32)
        scol = A("scol", [128, 4], F32)
        qint = A("qint", [128, DM], BF16)
        qiT = A("qiT", [128, 8 * 128], BF16)
        ftile = A("ftile", [128, DM], F32)

        psD0 = stk.enter_context(nc.psum_tensor("psD0", [128, 1024], F32))
        psD1 = stk.enter_context(nc.psum_tensor("psD1", [128, 1024], F32))
        psU0 = stk.enter_context(nc.psum_tensor("psU0", [128, 512], F32))
        psU1 = stk.enter_context(nc.psum_tensor("psU1", [128, 512], F32))
        psT = stk.enter_context(nc.psum_tensor("psT", [128, 512], BF16))
        psK = stk.enter_context(nc.psum_tensor("psK", [128, 512], F32))
        psD = [psD0, psD1]
        psU = [psU0, psU1]

        sems, cnt = {}, {}
        for s in ["sd", "sv", "ss", "stt", "sg"]:
            sems[s] = stk.enter_context(nc.semaphore(s))
            cnt[s] = 0

        blk = stk.enter_context(nc.Block())

        def I(instr, s, n):
            instr.then_inc(sems[s], n)
            cnt[s] += n
            return cnt[s]

        def W(eng, s, val=None):
            eng.wait_ge(sems[s], cnt[s] if val is None else val)

        SY, V, SC, T, G = nc.sync, nc.vector, nc.scalar, nc.tensor, nc.gpsimd

        def ln_tile(src, dst_bf16):
            W(V, "sv")
            W(SC, "sv")
            I(V.tensor_reduce(out=stat[:, 0:1], in_=src, axis=AxX, op=Alu.add), "sv", 1)
            W(SC, "sv")
            I(SC.activation(out=sq[:], in_=src, func=Act.Square, accum_out=stat[:, 1:2]), "ss", 1)
            W(V, "ss")
            I(V.tensor_scalar(out=stat[:, 2:3], in0=stat[:, 0:1], scalar1=1.0 / DM,
                              scalar2=None, op0=Alu.mult), "sv", 1)
            W(V, "sv")
            I(V.tensor_tensor(out=stat[:, 3:4], in0=stat[:, 2:3], in1=stat[:, 2:3], op=Alu.mult), "sv", 1)
            I(V.tensor_scalar(out=stat[:, 4:5], in0=stat[:, 1:2], scalar1=1.0 / DM,
                              scalar2=None, op0=Alu.mult), "sv", 1)
            W(V, "sv")
            I(V.tensor_tensor(out=stat[:, 5:6], in0=stat[:, 4:5], in1=stat[:, 3:4], op=Alu.subtract), "sv", 1)
            W(SC, "sv")
            I(SC.activation(out=stat[:, 6:7], in_=stat[:, 5:6], func=Act.Sqrt, bias=stat[:, 10:11], scale=1.0), "ss", 1)
            W(V, "ss")
            I(V.reciprocal(out=stat[:, 7:8], in_=stat[:, 6:7]), "sv", 1)
            W(V, "sv")
            I(V.tensor_scalar(out=dst_bf16, in0=src, scalar1=stat[:, 2:3], scalar2=stat[:, 7:8],
                              op0=Alu.subtract, op1=Alu.mult), "sv", 1)

        def transpose_block(src_ap, dst_ap, n_ident, width=128):
            W(T, "sv")
            I(T.transpose(psT[0:width, 0:n_ident], src_ap, ident[0:n_ident, 0:n_ident]), "stt", 1)
            W(V, "stt")
            I(V.tensor_copy(out=dst_ap, in_=psT[0:width, 0:n_ident]), "sv", 1)

        @blk.sync
        def _(_sy):
            I(V.memset(stat[:, 10:11], 1e-5), "sv", 1)
            I(V.memset(onec[:, 0:1], 1.0), "sv", 1)
            I(V.memset(onec[:, 1:2], -1.0), "sv", 1)
            W(SY, "sd")
            I(SY.dma_start(out=ident[:], in_=identin[:]), "sd", 16)
            W(SY, "sd")
            I(SY.dma_start(out=perm[:, 0:128], in_=permin[0:64, :]), "sd", 16)
            W(SY, "sd")
            I(SY.dma_start(out=perm[:, 128:256], in_=permin[64:128, :]), "sd", 16)
            W(SY, "sd")
            I(SY.dma_start(out=hmask[:], in_=bass.AP(hmaskin.tensor, 0, [[0, 64], [1, 2]])), "sd", 16)
            for i in range(64):
                W(SY, "sd")
                I(SY.dma_start(out=sel[:, i * 128:(i + 1) * 128],
                               in_=selin[i * 128:(i + 1) * 128, :]), "sd", 16)
            for oc in range(24):
                W(SY, "sd")
                I(SY.dma_start(out=biasq[:, oc:oc + 1],
                               in_=bass.AP(qkv_b.tensor, oc * 128, [[1, 128], [1, 1]])), "sd", 16)
            W(SY, "sd")
            I(SY.dma_start(out=wf32[:], in_=bass.AP(proj_b.tensor, 0, [[0, 128], [1, 1024]])), "sd", 16)
            W(V, "sd")
            I(V.tensor_copy(out=biasp[:], in_=wf32[:]), "sv", 1)
            W(SY, "sv")
            W(SY, "sd")
            I(SY.dma_start(out=wf32[:], in_=bass.AP(ffn_b.tensor, 0, [[0, 128], [1, 1024]])), "sd", 16)
            W(V, "sd")
            I(V.tensor_copy(out=biasf[:], in_=wf32[:]), "sv", 1)
            for h in range(16):
                W(SY, "sd")
                I(SY.dma_start(out=dwcw[:, h * 3:h * 3 + 3],
                               in_=bass.AP(dwc_w.tensor, h * 192, [[3, 64], [1, 3]])), "sd", 16)
                W(SY, "sd")
                I(SY.dma_start(out=dwcb[:, h:h + 1],
                               in_=bass.AP(dwc_b.tensor, h * 64, [[1, 64], [1, 1]])), "sd", 16)

            for oc in range(24):
                W(SY, "sv")
                W(SY, "sd")
                I(SY.dma_start(out=wf32[:], in_=qkv_w[oc * 128:(oc + 1) * 128, :]), "sd", 16)
                W(V, "sd")
                W(V, "sd")
                I(V.tensor_copy(out=qwtile[:], in_=wf32[:]), "sv", 1)
                for dmc in range(8):
                    transpose_block(qwtile[:, dmc * 128:(dmc + 1) * 128],
                                    wqTb[:, dmc * 128:(dmc + 1) * 128], 128)
                W(SY, "sv")
                W(SY, "sd")
                I(SY.dma_start(out=bass.AP(wq2.tensor, oc * 128,
                                           [[3 * DM, 128], [128 * 3 * DM, 8], [1, 128]]),
                               in_=wqTb[:, 0:1024]), "sd", 16)
            for oc in range(8):
                W(SY, "sv")
                W(SY, "sd")
                I(SY.dma_start(out=wf32[:], in_=proj_w[oc * 128:(oc + 1) * 128, :]), "sd", 16)
                W(V, "sd")
                I(V.tensor_copy(out=qwtile[:], in_=wf32[:]), "sv", 1)
                for dmc in range(8):
                    transpose_block(qwtile[:, dmc * 128:(dmc + 1) * 128],
                                    wpT[:, dmc * 1024 + oc * 128: dmc * 1024 + (oc + 1) * 128], 128)

            wq_loaded = [None, None]
            wq_used = [None, None]
            for ch in range(8):
                for tt in range(4):
                    t0 = ch * 4 + tt
                    W(SY, "sv")
                    W(SY, "sd")
                    I(SY.dma_start(out=xtile[:], in_=xb[t0 * 128:(t0 + 1) * 128, :]), "sd", 16)
                    W(V, "sd")
                    W(SC, "sd")
                    ln_tile(xtile[:], xnt[:])
                    for dmc in range(8):
                        transpose_block(xnt[:, dmc * 128:(dmc + 1) * 128],
                                        xnT[:, dmc * 512 + tt * 128: dmc * 512 + (tt + 1) * 128], 128)
                ocs = list(range(8, 24)) + (list(range(0, 8)) if ch < 4 else [])
                for j_oc, oc in enumerate(ocs):
                    par = j_oc % 2
                    if j_oc >= 2:
                        W(SY, "stt", wq_used[par])
                    W(SY, "sd")
                    I(SY.dma_start(out=wqTb[:, par * 1024:(par + 1) * 1024],
                                   in_=bass.AP(wq2.tensor, oc * 128,
                                               [[3 * DM, 128], [128 * 3 * DM, 8], [1, 128]])), "sd", 16)
                    wq_loaded[par] = cnt["sd"]
                    W(T, "sv")
                    W(T, "ss")
                    W(T, "sd", wq_loaded[par])
                    mm = None
                    for dmc in range(8):
                        mm = T.matmul(out=psK[:, 0:512],
                                      lhsT=wqTb[:, par * 1024 + dmc * 128: par * 1024 + (dmc + 1) * 128],
                                      rhs=xnT[:, dmc * 512:(dmc + 1) * 512],
                                      start=(dmc == 0), stop=(dmc == 7))
                    wq_used[par] = I(mm, "stt", 1)
                    W(V, "stt")
                    W(V, "sd")
                    I(V.tensor_scalar(out=qkvtile[:], in0=psK[:, 0:512], scalar1=biasq[:, oc:oc + 1],
                                      scalar2=None, op0=Alu.add), "sv", 1)
                    W(SY, "sv")
                    W(SY, "sd")
                    I(SY.dma_start(out=qkvs[oc * 128:(oc + 1) * 128, ch * 512:(ch + 1) * 512],
                                   in_=qkvtile[:]), "sd", 16)

            for h in range(H):
                W(SY, "sv")
                W(SY, "stt")
                W(SY, "sd")
                I(SY.dma_start(out=kT[:], in_=qkvs[1024 + h * 64:1024 + (h + 1) * 64, :]), "sd", 16)
                W(SY, "sd")
                I(SY.dma_start(out=vT[:], in_=qkvs[2048 + h * 64:2048 + (h + 1) * 64, :]), "sd", 16)
                W(SY, "sd")
                I(SY.dma_start(out=qTh[:], in_=qkvs[h * 64:(h + 1) * 64, 0:OWN]), "sd", 16)

                W(V, "sd")
                for kg in range(NKG):
                    W(V, "stt")
                    I(V.tensor_copy(out=vpair[:, 0:64], in_=vT[:, kg * 64:(kg + 1) * 64]), "sv", 1)
                    I(V.tensor_copy(out=vpair[:, 64:128], in_=vT[:, 2048 + kg * 64:2048 + (kg + 1) * 64]), "sv", 1)
                    W(T, "sv")
                    W(T, "sd")
                    I(T.transpose(psT[:, 0:64], vpair[:], ident[0:64, 0:64]), "stt", 1)
                    W(V, "stt")
                    I(V.tensor_copy(out=vtok[:, kg * 65:kg * 65 + 64], in_=psT[:, 0:64]), "sv", 1)
                I(V.memset(bass.AP(vtok, 64, [[NKG * 65, 128], [65, NKG]]), 1.0), "sv", 1)

                w0 = dwcw[:, h * 3 + 0:h * 3 + 1]
                w1 = dwcw[:, h * 3 + 1:h * 3 + 2]
                w2 = dwcw[:, h * 3 + 2:h * 3 + 3]
                I(V.tensor_tensor(out=stat[0:64, 8:9], in0=w0, in1=hmask[:, 0:1], op=Alu.mult), "sv", 1)
                I(V.tensor_tensor(out=stat[0:64, 9:10], in0=w2, in1=hmask[:, 1:2], op=Alu.mult), "sv", 1)
                for hf in range(2):
                    c0 = hf * 1024
                    W(V, "sv")
                    I(V.tensor_scalar(out=cv1[:], in0=vT[:, c0:c0 + 1024], scalar1=w1, scalar2=None,
                                      op0=Alu.mult), "sv", 1)
                    W(V, "sv")
                    I(V.scalar_tensor_tensor(out=cv2[:, 1:1024], in0=vT[:, c0:c0 + 1023], scalar=w0,
                                             in1=cv1[:, 1:1024], op0=Alu.mult, op1=Alu.add), "sv", 1)
                    if hf == 0:
                        I(V.scalar_tensor_tensor(out=cv2[:, 0:1], in0=vT[:, S - 1:S],
                                                 scalar=stat[0:64, 8:9],
                                                 in1=cv1[:, 0:1], op0=Alu.mult, op1=Alu.add), "sv", 1)
                    else:
                        I(V.scalar_tensor_tensor(out=cv2[:, 0:1], in0=vT[:, c0 - 1:c0], scalar=w0,
                                                 in1=cv1[:, 0:1], op0=Alu.mult, op1=Alu.add), "sv", 1)
                    W(V, "sv")
                    I(V.scalar_tensor_tensor(out=cv1[:, 0:1023], in0=vT[:, c0 + 1:c0 + 1024], scalar=w2,
                                             in1=cv2[:, 0:1023], op0=Alu.mult, op1=Alu.add), "sv", 1)
                    if hf == 1:
                        I(V.scalar_tensor_tensor(out=cv1[:, 1023:1024], in0=vT[:, OWN:OWN + 1],
                                                 scalar=stat[0:64, 9:10],
                                                 in1=cv2[:, 1023:1024], op0=Alu.mult, op1=Alu.add), "sv", 1)
                    else:
                        I(V.scalar_tensor_tensor(out=cv1[:, 1023:1024], in0=vT[:, c0 + 1024:c0 + 1025],
                                                 scalar=w2,
                                                 in1=cv2[:, 1023:1024], op0=Alu.mult, op1=Alu.add), "sv", 1)
                    W(V, "sv")
                    I(V.tensor_scalar(out=convb[:, c0:c0 + 1024], in0=cv1[:], scalar1=dwcb[:, h:h + 1],
                                      scalar2=None, op0=Alu.add), "sv", 1)
                for j in range(NQT):
                    transpose_block(convb[:, j * 128:(j + 1) * 128], ltok[:, j * 64:(j + 1) * 64],
                                    64, width=128)

                for chk in range(8):
                    W(T, "sv"); W(T, "ss")
                    mm = T.matmul(out=psK[0:1, 0:512], lhsT=onec[:, 1:2],
                                  rhs=kT[:, chk * 512:(chk + 1) * 512], start=True, stop=True)
                    I(mm, "stt", 1)
                    W(SC, "stt")
                    W(SC, "sd")
                    I(SC.activation(out=usb[0:1, 0:512], in_=psK[0:1, 0:512], func=Act.Copy), "ss", 1)
                    W(SY, "ss")
                    W(SY, "sd")
                    I(SY.dma_start(out=ksd[chk * 512:(chk + 1) * 512], in_=usb[0:1, 0:512]), "sd", 16)
                for chk in range(4):
                    W(T, "sv"); W(T, "ss")
                    mm = T.matmul(out=psK[0:1, 0:512], lhsT=onec[:, 0:1],
                                  rhs=qTh[:, chk * 512:(chk + 1) * 512], start=True, stop=True)
                    I(mm, "stt", 1)
                    W(SC, "stt")
                    W(SC, "sd")
                    I(SC.activation(out=usb[0:1, 0:512], in_=psK[0:1, 0:512], func=Act.Copy), "ss", 1)
                    W(SY, "ss")
                    W(SY, "sd")
                    I(SY.dma_start(out=qsd[chk * 512:(chk + 1) * 512], in_=usb[0:1, 0:512]), "sd", 16)
                W(V, "stt")
                I(V.memset(qs2[0:2, :], 1.0), "sv", 1)
                I(V.memset(kstall[0:2, :], 1.0), "sv", 1)
                W(V, "sv")
                I(V.memset(bass.AP(kstall, 0, [[32 * 128, 2], [128, NKG], [1, 32]]), 0.0), "sv", 1)
                W(SY, "sv"); W(SY, "sd")
                W(SY, "sd")
                I(SY.dma_start(out=qs2[1:2, :], in_=bass.AP(qsd.tensor, 0, [[0, 1], [1, OWN]])), "sd", 16)
                W(SY, "stt")
                for kg2 in range(NKG):
                    W(SY, "sd")
                    I(SY.dma_start(out=kstall[0:1, kg2 * 128 + 32:kg2 * 128 + 80],
                                   in_=bass.AP(ksd.tensor, kg2 * 64 + 16,
                                               [[0, 1], [1, 48]])), "sd", 16)
                    W(SY, "sd")
                    I(SY.dma_start(out=kstall[0:1, kg2 * 128 + 80:kg2 * 128 + 128],
                                   in_=bass.AP(ksd.tensor, 2048 + kg2 * 64 + 16,
                                               [[0, 1], [1, 48]])), "sd", 16)
                kst_ready = cnt["sd"]

                for kg in range(NKG):
                    W(T, "sv")
                    W(T, "ss")
                    T.matmul(out=psK[:, 0:64], lhsT=perm[:, 0:128],
                             rhs=kT[:, kg * 64:(kg + 1) * 64], start=True, stop=False)
                    mm = T.matmul(out=psK[:, 0:64], lhsT=perm[:, 128:256],
                                  rhs=kT[:, 2048 + kg * 64:2048 + (kg + 1) * 64],
                                  start=False, stop=True)
                    I(mm, "stt", 1)
                    W(V, "stt")
                    I(V.tensor_copy(out=kdup[:, kg * 64:(kg + 1) * 64], in_=psK[:, 0:64]), "sv", 1)

                for qh2 in range(2):
                    for qc in range(2):
                        W(T, "sv")
                        W(T, "ss")
                        T.matmul(out=psK[:, 0:512], lhsT=perm[:, 0:128],
                                 rhs=qTh[:, qh2 * 1024 + qc * 512: qh2 * 1024 + (qc + 1) * 512],
                                 start=True, stop=False)
                        mm = T.matmul(out=psK[:, 0:512], lhsT=perm[:, 128:256],
                                      rhs=qTh[:, qh2 * 1024 + qc * 512: qh2 * 1024 + (qc + 1) * 512],
                                      start=False, stop=True)
                        I(mm, "stt", 1)
                        W(V, "stt")
                        I(V.tensor_copy(out=qdup[:, qc * 512:(qc + 1) * 512], in_=psK[:, 0:512]), "sv", 1)

                    slot_free = [None] * RING
                    prod_cnt = [None] * RING
                    prod_sem = [None] * RING
                    exp_hist = {}
                    umm_hist = {}
                    W(V, "sv")
                    W(SC, "sv")
                    W(SC, "stt")
                    for kg in range(NKG):
                        for ordi in range(64):
                            i = 16 + ordi if ordi < 48 else ordi - 48
                            gidx = kg * 64 + ordi
                            slot = gidx % RING
                            is_abs = (i < 16)
                            eng = SC if is_abs else V
                            if gidx % GRP == 0:
                                tgt = slot_free[(gidx + GRP - 1) % RING]
                                if tgt is not None:
                                    W(eng, "stt", tgt)
                            kcol = kg * 64 + i
                            if is_abs:
                                pr = SC.activation(out=tbuf[:, slot * 1024:(slot + 1) * 1024],
                                                   in_=qdup[:], func=Act.Abs, scale=-1.0,
                                                   bias=kdup[:, kcol:kcol + 1])
                                prod_cnt[slot] = I(pr, "ss", 1)
                                prod_sem[slot] = "ss"
                            else:
                                ts = V.tensor_scalar(out=tbuf[:, slot * 1024:(slot + 1) * 1024],
                                                     in0=qdup[:], scalar1=kdup[:, kcol:kcol + 1],
                                                     scalar2=0.0, op0=Alu.subtract, op1=Alu.min)
                                prod_cnt[slot] = I(ts, "sv", 1)
                                prod_sem[slot] = "sv"
                            if gidx % GRP == 0:
                                W(T, prod_sem[slot], prod_cnt[slot] + GRP - 1)
                            if ordi == 0:
                                if kg >= 2:
                                    W(T, "ss", exp_hist[kg - 2])
                                elif kg < 2:
                                    W(T, "ss")
                            mm = None
                            for qc in range(2):
                                mm = T.matmul(out=psD[kg % 2][:, qc * 512:(qc + 1) * 512],
                                              lhsT=sel[:, i * 128:(i + 1) * 128],
                                              rhs=tbuf[:, slot * 1024 + qc * 512: slot * 1024 + (qc + 1) * 512],
                                              start=(ordi == 0), stop=(ordi == 63))
                            slot_free[slot] = I(mm, "stt", 1)
                        if kg == 0 and qh2 == 0:
                            W(T, "sd", kst_ready)
                        mm = None
                        for qc in range(2):
                            mm = T.matmul(out=psD[kg % 2][:, qc * 512:(qc + 1) * 512],
                                          lhsT=kstall[:, kg * 128:(kg + 1) * 128],
                                          rhs=qs2[:, qh2 * 1024 + qc * 512: qh2 * 1024 + (qc + 1) * 512],
                                          start=False, stop=False)
                        I(mm, "stt", 1)
                        W(SC, "stt")
                        if kg >= 2:
                            W(SC, "stt", umm_hist[kg - 2])
                        I(SC.activation(out=kern[0:32, (kg % 2) * 1024:(kg % 2 + 1) * 1024],
                                        in_=psD[kg % 2][0:32, :], func=Act.Exp,
                                        scale=0.03125), "ss", 1)
                        I(SC.activation(out=kern[32:64, (kg % 2) * 1024:(kg % 2 + 1) * 1024],
                                        in_=psD[kg % 2][32:64, :], func=Act.Exp,
                                        scale=-0.0625), "ss", 1)
                        a = SC.activation(out=kern[64:128, (kg % 2) * 1024:(kg % 2 + 1) * 1024],
                                          in_=psD[kg % 2][64:128, :], func=Act.Exp,
                                          scale=-0.0625)
                        exp_hist[kg] = I(a, "ss", 1)
                        if kg >= 1:
                            pk = (kg - 1) % 2
                            if kg == 1:
                                W(T, "ss")
                            else:
                                W(T, "ss", exp_hist[kg - 1])
                            mm = None
                            for qc in range(2):
                                mm = T.matmul(out=psU[qc][0:65, :],
                                              lhsT=vtok[:, (kg - 1) * 65:(kg - 1) * 65 + 65],
                                              rhs=kern[:, pk * 1024 + qc * 512: pk * 1024 + (qc + 1) * 512],
                                              start=(kg - 1 == 0), stop=False)
                            umm_hist[kg - 1] = I(mm, "stt", 1)
                    W(T, "ss", exp_hist[31])
                    mm = None
                    for qc in range(2):
                        mm = T.matmul(out=psU[qc][0:65, :],
                                      lhsT=vtok[:, 31 * 65:31 * 65 + 65],
                                      rhs=kern[:, (31 % 2) * 1024 + qc * 512: (31 % 2) * 1024 + (qc + 1) * 512],
                                      start=False, stop=True)
                    umm_hist[31] = I(mm, "stt", 1)

                    for qc in range(2):
                        W(SC, "stt")
                        I(SC.activation(out=usb[:], in_=psU[qc][0:65, :], func=Act.Copy), "ss", 1)
                        for j in range(4):
                            jg = qh2 * 8 + qc * 4 + j
                            W(T, "ss")
                            W(T, "sv")
                            I(T.transpose(psT[:, 0:65], usb[:, j * 128:(j + 1) * 128],
                                          ident[0:65, 0:65]), "stt", 1)
                            W(V, "stt")
                            I(V.tensor_copy(out=utok[:, j * 65:(j + 1) * 65],
                                            in_=psT[:, 0:65]), "sv", 1)
                            W(V, "sv")
                            I(V.tensor_scalar(out=den[:, j:j + 1],
                                              in0=utok[:, j * 65 + 64:j * 65 + 65],
                                              scalar1=1e-6, scalar2=None, op0=Alu.add), "sv", 1)
                            W(V, "sv")
                            I(V.reciprocal(out=den[:, j:j + 1], in_=den[:, j:j + 1]), "sv", 1)
                            W(V, "sv")
                            I(V.scalar_tensor_tensor(
                                out=ytok[:, jg * DM + h * 64: jg * DM + (h + 1) * 64],
                                in0=utok[:, j * 65:j * 65 + 64], scalar=den[:, j:j + 1],
                                in1=ltok[:, jg * 64:(jg + 1) * 64],
                                op0=Alu.mult, op1=Alu.add), "sv", 1)

            I(V.memset(mcol[:, 0:1], 0.0), "sv", 1)
            for tt in range(NQT):
                for cc in range(8):
                    transpose_block(ytok[:, tt * DM + cc * 128: tt * DM + (cc + 1) * 128],
                                    yTt[:, cc * 128:(cc + 1) * 128], 128)
                for o2c in range(2):
                    W(T, "sv")
                    mm = None
                    for cc in range(8):
                        mm = T.matmul(out=psD[o2c][:, 0:512],
                                      lhsT=yTt[:, cc * 128:(cc + 1) * 128],
                                      rhs=wpT[:, cc * 1024 + o2c * 512: cc * 1024 + (o2c + 1) * 512],
                                      start=(cc == 0), stop=(cc == 7))
                    I(mm, "stt", 1)
                W(V, "stt")
                for o2c in range(2):
                    I(V.tensor_tensor(out=ptile[:, o2c * 512:(o2c + 1) * 512],
                                      in0=psD[o2c][:, 0:512],
                                      in1=biasp[:, o2c * 512:(o2c + 1) * 512], op=Alu.add), "sv", 1)
                ln_tile(ptile[:], xnt[:])
                W(SY, "sv")
                W(SY, "sd")
                I(SY.dma_start(out=xtile[:], in_=xb[tt * 128:(tt + 1) * 128, :]), "sd", 16)
                W(V, "sd")
                I(V.tensor_tensor(out=ztile[:], in0=xtile[:], in1=xnt[:], op=Alu.add), "sv", 1)
                W(SY, "sv")
                W(SY, "sd")
                I(SY.dma_start(out=zs[tt * 128:(tt + 1) * 128, :], in_=ztile[:]), "sd", 16)
                ln_tile(ztile[:], xft[:])
                W(V, "sv")
                I(V.tensor_reduce(out=mcol[:, 1:2], in_=xft[:], axis=AxX,
                                  op=Alu.max, apply_absolute_value=True), "sv", 1)
                W(V, "sv")
                I(V.tensor_tensor(out=mcol[:, 0:1], in0=mcol[:, 0:1], in1=mcol[:, 1:2],
                                  op=Alu.max), "sv", 1)
                W(SY, "sv")
                W(SY, "sd")
                I(SY.dma_start(out=xfs[tt * 128:(tt + 1) * 128, :], in_=xft[:]), "sd", 16)

            I(V.memset(mcol[:, 2:3], 0.0), "sv", 1)
            for oc in range(8):
                W(SY, "sv")
                W(SY, "sd")
                I(SY.dma_start(out=wf32[:], in_=ffn_w[oc * 128:(oc + 1) * 128, :]), "sd", 16)
                W(V, "sd")
                I(V.tensor_reduce(out=mcol[:, 3:4], in_=wf32[:], axis=AxX, op=Alu.add,
                                  apply_absolute_value=True), "sv", 1)
                W(V, "sv")
                I(V.tensor_tensor(out=mcol[:, 2:3], in0=mcol[:, 2:3], in1=mcol[:, 3:4],
                                  op=Alu.add), "sv", 1)
            W(V, "sv")
            I(V.tensor_copy(out=qwtile[:, 0:1], in_=mcol[:, 0:1]), "sv", 1)
            I(V.tensor_copy(out=qwtile[:, 1:2], in_=mcol[:, 2:3]), "sv", 1)
            transpose_block(qwtile[:, 0:1], sq[0:1, 0:128], 128, width=1)
            transpose_block(qwtile[:, 1:2], sq[0:1, 128:256], 128, width=1)
            W(V, "sv")
            I(V.tensor_reduce(out=scal[0:1, 0:1], in_=sq[0:1, 0:128], axis=AxX, op=Alu.max), "sv", 1)
            I(V.tensor_reduce(out=scal[0:1, 1:2], in_=sq[0:1, 128:256], axis=AxX, op=Alu.add), "sv", 1)
            W(V, "sv")
            I(V.tensor_scalar(out=scal[0:1, 1:2], in0=scal[0:1, 1:2], scalar1=float(2.0 ** -20),
                              scalar2=None, op0=Alu.mult), "sv", 1)
            W(V, "sv")
            I(V.tensor_scalar(out=scal[0:1, 2:3], in0=scal[0:1, 0:1], scalar1=float(2.0 ** 18),
                              scalar2=None, op0=Alu.mult), "sv", 1)
            W(V, "sv")
            I(V.tensor_copy(out=ftile[0:1, 0:1].bitcast(I32), in_=scal[0:1, 2:3]), "sv", 1)
            W(V, "sv")
            I(V.tensor_copy(out=scal[0:1, 2:3], in_=ftile[0:1, 0:1].bitcast(I32)), "sv", 1)
            W(V, "sv")
            I(V.scalar_tensor_tensor(out=scal[0:1, 3:4], in0=scal[0:1, 2:3],
                                     scalar=float(-(2.0 ** -18)), in1=scal[0:1, 0:1],
                                     op0=Alu.mult, op1=Alu.add), "sv", 1)
            W(V, "sv")
            I(V.tensor_scalar(out=scal[0:1, 4:5], in0=scal[0:1, 3:4], scalar1=float(2.0 ** 42),
                              scalar2=None, op0=Alu.mult), "sv", 1)
            W(V, "sv")
            I(V.tensor_copy(out=ftile[0:1, 1:2].bitcast(I32), in_=scal[0:1, 4:5]), "sv", 1)
            W(V, "sv")
            I(V.tensor_copy(out=scal[0:1, 4:5], in_=ftile[0:1, 1:2].bitcast(I32)), "sv", 1)
            W(V, "sv")
            I(V.tensor_copy(out=scal[0:1, 5:6], in_=scal[0:1, 2:3]), "sv", 1)
            I(V.tensor_copy(out=scal[0:1, 6:7], in_=scal[0:1, 4:5]), "sv", 1)
            W(SC, "sv")
            W(SC, "ss")
            I(SC.dma_start(out=cc_in[:], in_=scal[0:1, 5:7]), "ss", 16)
            W(G, "ss")
            I(G.collective_compute("AllGather", Alu.bypass, ins=[cc_in[:]], outs=[cc_out[:]],
                                   replica_groups=[[0, 1, 2, 3, 4, 5, 6, 7]]), "sg", 1)
            W(SC, "sg")
            W(SC, "ss")
            I(SC.dma_start(out=scal[0:1, 8:24],
                           in_=bass.AP(cc_out.tensor, 0, [[16, 1], [1, 16]])), "ss", 16)
            W(V, "ss")
            h_view = bass.AP(scal, 8, [[32, 1], [2, 8]])
            l_view = bass.AP(scal, 9, [[32, 1], [2, 8]])
            I(V.tensor_scalar(out=scal[0:1, 24:32], in0=h_view, scalar1=float(2.0 ** -18),
                              scalar2=None, op0=Alu.mult), "sv", 1)
            W(V, "sv")
            I(V.scalar_tensor_tensor(out=scal[0:1, 8:16], in0=l_view, scalar=float(2.0 ** -42),
                                     in1=scal[0:1, 24:32], op0=Alu.mult, op1=Alu.add), "sv", 1)
            W(V, "sv")
            I(V.tensor_reduce(out=scal[0:1, 16:17], in_=scal[0:1, 8:16], axis=AxX, op=Alu.max), "sv", 1)
            W(V, "sv")
            I(V.reciprocal(out=scal[0:1, 17:18], in_=scal[0:1, 16:17]), "sv", 1)
            W(V, "sv")
            I(V.tensor_scalar(out=scal[0:1, 18:19], in0=scal[0:1, 17:18], scalar1=127.0,
                              scalar2=None, op0=Alu.mult), "sv", 1)
            I(V.tensor_scalar(out=scal[0:1, 19:20], in0=scal[0:1, 16:17],
                              scalar1=float(1.0 / 127.0), scalar2=None, op0=Alu.mult), "sv", 1)
            W(V, "sv")
            I(V.tensor_tensor(out=scal[0:1, 20:21], in0=scal[0:1, 19:20], in1=scal[0:1, 1:2],
                              op=Alu.mult), "sv", 1)
            I(V.tensor_scalar(out=scal[0:1, 21:22], in0=scal[0:1, 1:2], scalar1=0.5,
                              scalar2=None, op0=Alu.mult), "sv", 1)
            W(SY, "sv")
            W(SY, "sd")
            I(SY.dma_start(out=sc_d[0:4], in_=scal[0:1, 18:22]), "sd", 16)
            W(SY, "sd")
            W(SY, "sd")
            I(SY.dma_start(out=scol[:, 0:4],
                           in_=bass.AP(sc_d.tensor, 0, [[0, 128], [1, 4]])), "sd", 16)

            for oc in range(8):
                W(SY, "sv")
                W(SY, "ss")
                W(SY, "sd")
                I(SY.dma_start(out=wf32[:], in_=ffn_w[oc * 128:(oc + 1) * 128, :]), "sd", 16)
                W(SC, "sd")
                I(SC.activation(out=ptile[:], in_=wf32[:], func=Act.Abs), "ss", 1)
                W(SC, "ss")
                I(SC.activation(out=ztile[:], in_=wf32[:], func=Act.Sign), "ss", 1)
                W(V, "ss")
                W(V, "sv")
                I(V.tensor_scalar(out=ftile[:], in0=ptile[:], scalar1=scol[:, 3:4], scalar2=None,
                                  op0=Alu.is_gt), "sv", 1)
                W(V, "sv")
                W(V, "ss")
                I(V.tensor_tensor(out=qwtile[:], in0=ftile[:], in1=ztile[:], op=Alu.mult), "sv", 1)
                for kc in range(8):
                    transpose_block(qwtile[:, kc * 128:(kc + 1) * 128],
                                    yTt[:, kc * 128:(kc + 1) * 128], 128)
                W(SY, "sv")
                W(SY, "sd")
                I(SY.dma_start(out=bass.AP(qw2.tensor, oc * 128,
                                           [[DM, 128], [128 * DM, 8], [1, 128]]),
                               in_=yTt[:, 0:1024]), "sd", 16)

            qw_loaded = [None] * 4
            qw_used = [None] * 4
            for tt in range(NQT):
                W(SY, "sv")
                W(SY, "sd")
                I(SY.dma_start(out=xft[:], in_=xfs[tt * 128:(tt + 1) * 128, :]), "sd", 16)
                W(V, "stt")
                W(V, "sd")
                I(V.tensor_scalar(out=ftile[:], in0=xft[:],
                                  scalar1=scol[:, 0:1], scalar2=None, op0=Alu.mult), "sv", 1)
                W(SC, "sv")
                I(SC.activation(out=sq[:], in_=ftile[:], func=Act.Sign), "ss", 1)
                W(V, "ss")
                I(V.scalar_tensor_tensor(out=ptile[:], in0=sq[:], scalar=0.5, in1=ftile[:],
                                         op0=Alu.mult, op1=Alu.add), "sv", 1)
                W(V, "sv")
                I(V.tensor_copy(out=ftile[:].bitcast(I32), in_=ptile[:]), "sv", 1)
                W(V, "sv")
                I(V.tensor_copy(out=qint[:], in_=ftile[:].bitcast(I32)), "sv", 1)
                for kc in range(8):
                    transpose_block(qint[:, kc * 128:(kc + 1) * 128],
                                    qiT[:, kc * 128:(kc + 1) * 128], 128)
                for oc2 in range(2):
                    for kc in range(8):
                        slot = kc % 4
                        if qw_used[slot] is not None:
                            W(SY, "stt", qw_used[slot])
                        W(SY, "sd")
                        I(SY.dma_start(out=wqTb[:, slot * 512:(slot + 1) * 512],
                                       in_=bass.AP(qw2.tensor, kc * 128 * DM + oc2 * 512,
                                                   [[DM, 128], [1, 512]])), "sd", 16)
                        qw_loaded[slot] = cnt["sd"]
                        if kc == 0:
                            W(T, "sv")
                        W(T, "sd", qw_loaded[slot])
                        mm = T.matmul(out=psD[oc2][:, 0:512],
                                      lhsT=qiT[:, kc * 128:(kc + 1) * 128],
                                      rhs=wqTb[:, slot * 512:(slot + 1) * 512],
                                      start=(kc == 0), stop=(kc == 7))
                        qw_used[slot] = I(mm, "stt", 1)
                W(V, "stt")
                for oc2 in range(2):
                    I(V.scalar_tensor_tensor(out=ftile[:, oc2 * 512:(oc2 + 1) * 512],
                                             in0=psD[oc2][:, 0:512], scalar=scol[:, 2:3],
                                             in1=biasf[:, oc2 * 512:(oc2 + 1) * 512],
                                             op0=Alu.mult, op1=Alu.add), "sv", 1)
                ln_tile(ftile[:], xnt[:])
                W(SY, "sv")
                W(SY, "sd")
                I(SY.dma_start(out=ztile[:], in_=zs[tt * 128:(tt + 1) * 128, :]), "sd", 16)
                W(V, "sd")
                I(V.tensor_copy(out=ptile[:], in_=xnt[:]), "sv", 1)
                W(V, "sv")
                I(V.tensor_tensor(out=ftile[:], in0=ztile[:], in1=ptile[:], op=Alu.add), "sv", 1)
                W(SY, "sv")
                W(SY, "sd")
                I(SY.dma_start(out=out[tt * 128:(tt + 1) * 128, :], in_=ftile[:]), "sd", 16)

            W(SY, "sd")

    nc.finalize()
    return nc


def _host_constants():
    selv = np.zeros((64 * 128, 128), dtype=np.float32)
    for i in range(64):
        if i < 16:
            ra, rb = i, 16 + i
        else:
            ra, rb = 32 + (i - 16), 80 + (i - 16)
        selv[i * 128: i * 128 + 64, ra] = -2.0
        selv[i * 128 + 64: i * 128 + 128, rb] = -2.0
    permv = np.zeros((128, 128), dtype=np.float32)
    for d in range(64):
        permv[d, d] = 1.0
        permv[64 + d, 64 + d] = 1.0
    identv = np.eye(128, dtype=np.float32)
    return selv.astype(BB), permv.astype(BB), identv.astype(BB)


_NC_CACHE = None


def kernel(L0_attn_x, L0_attn_qkv_w, L0_attn_qkv_b, L0_attn_proj_w, L0_attn_proj_b,
           L0_attn_dwc_w, L0_attn_dwc_b, L0_ffn_weight, L0_ffn_bias):
    global _NC_CACHE
    if _NC_CACHE is None:
        _NC_CACHE = build_program()
    nc = _NC_CACHE

    x = np.asarray(L0_attn_x, dtype=np.float32)
    selv, permv, identv = _host_constants()
    common = dict(
        qkv_w=np.ascontiguousarray(L0_attn_qkv_w, dtype=np.float32),
        qkv_b=np.ascontiguousarray(L0_attn_qkv_b, dtype=np.float32),
        proj_w=np.ascontiguousarray(L0_attn_proj_w, dtype=np.float32),
        proj_b=np.ascontiguousarray(L0_attn_proj_b, dtype=np.float32),
        dwc_w=np.ascontiguousarray(np.asarray(L0_attn_dwc_w, dtype=np.float32).reshape(DM, 3)),
        dwc_b=np.ascontiguousarray(L0_attn_dwc_b, dtype=np.float32),
        ffn_w=np.ascontiguousarray(L0_ffn_weight, dtype=np.float32),
        ffn_b=np.ascontiguousarray(L0_ffn_bias, dtype=np.float32),
        selin=selv, permin=permv, identin=identv,
    )
    in_maps = []
    for c in range(8):
        b, qh = c // 2, c % 2
        m = dict(common)
        m["xb"] = np.ascontiguousarray(np.roll(x[b], -qh * OWN, axis=0))
        m["hmaskin"] = np.array([1.0 if qh == 1 else 0.0, 1.0 if qh == 0 else 0.0],
                                dtype=np.float32)
        in_maps.append(m)

    res = run_bass_kernel_spmd(nc, in_maps, core_ids=list(range(8)))
    outp = np.empty((B, S, DM), dtype=np.float32)
    for c in range(8):
        b, qh = c // 2, c % 2
        outp[b, qh * OWN:(qh + 1) * OWN] = res.results[c]["out"]
    return outp


if __name__ == "__main__":
    import time
    t0 = time.time()
    prog = build_program()
    n = sum(1 for _ in prog.all_instructions())
    print(f"built in {time.time()-t0:.1f}s, instructions: {n}")


# version 17
# speedup vs baseline: 1.4428x; 1.3706x over previous
import numpy as np
import ml_dtypes

import concourse.bass as bass
import concourse.mybir as mybir
from concourse.bass_utils import run_bass_kernel_spmd

F32 = mybir.dt.float32
BF16 = mybir.dt.bfloat16
I32 = mybir.dt.int32
Alu = mybir.AluOpType
Act = mybir.ActivationFunctionType
AxX = mybir.AxisListType.X
BB = ml_dtypes.bfloat16

B, S, DM, H, D = 4, 4096, 1024, 16, 64
OWN = 2048
NKG = 32
NQT = OWN // 128
RING = 11
GRP = 4


def build_program():
    nc = bass.Bass("TRN2", target_bir_lowering=False)

    xb = nc.dram_tensor("xb", [S, DM], F32, kind="ExternalInput").ap()
    qkv_w = nc.dram_tensor("qkv_w", [3 * DM, DM], F32, kind="ExternalInput").ap()
    qkv_b = nc.dram_tensor("qkv_b", [3 * DM], F32, kind="ExternalInput").ap()
    proj_w = nc.dram_tensor("proj_w", [DM, DM], F32, kind="ExternalInput").ap()
    proj_b = nc.dram_tensor("proj_b", [DM], F32, kind="ExternalInput").ap()
    dwc_w = nc.dram_tensor("dwc_w", [DM, 3], F32, kind="ExternalInput").ap()
    dwc_b = nc.dram_tensor("dwc_b", [DM], F32, kind="ExternalInput").ap()
    ffn_w = nc.dram_tensor("ffn_w", [DM, DM], F32, kind="ExternalInput").ap()
    ffn_b = nc.dram_tensor("ffn_b", [DM], F32, kind="ExternalInput").ap()
    selin = nc.dram_tensor("selin", [64 * 128, 128], BF16, kind="ExternalInput").ap()
    permin = nc.dram_tensor("permin", [128, 128], BF16, kind="ExternalInput").ap()
    identin = nc.dram_tensor("identin", [128, 128], BF16, kind="ExternalInput").ap()
    hmaskin = nc.dram_tensor("hmaskin", [2], F32, kind="ExternalInput").ap()

    out = nc.dram_tensor("out", [OWN, DM], F32, kind="ExternalOutput").ap()

    qkvs = nc.dram_tensor("qkvs", [3 * DM, S], BF16).ap()
    wq2 = nc.dram_tensor("wq2", [DM, 3 * DM], BF16).ap()
    xfs = nc.dram_tensor("xfs", [OWN, DM], BF16).ap()
    qw2 = nc.dram_tensor("qw2", [DM, DM], BF16).ap()
    ksd = nc.dram_tensor("ksd", [S], BF16).ap()
    qsd = nc.dram_tensor("qsd", [OWN], BF16).ap()
    zs = nc.dram_tensor("zs", [OWN, DM], F32).ap()
    sc_d = nc.dram_tensor("sc_d", [8], F32).ap()
    cc_in = nc.dram_tensor("cc_in", [1, 2], F32).ap()
    cc_out = nc.dram_tensor("cc_out", [8, 2], F32).ap()

    import contextlib
    stk = contextlib.ExitStack()
    with stk:
        A = lambda nm, shp, dt: stk.enter_context(nc.sbuf_tensor(nm, shp, dt))
        sel = A("sel", [128, 64 * 128], BF16)
        perm = A("perm", [64, 2 * 128], BF16)
        ident = A("ident", [128, 128], BF16)
        hmask = A("hmask", [64, 2], F32)
        wqTb = A("wqTb", [128, 2 * 1024], BF16)
        wpT = A("wpT", [128, 8 * 1024], BF16)
        wf32 = A("wf32", [128, 1024], F32)
        qwtile = A("qwtile", [128, 1024], BF16)
        biasq = A("biasq", [128, 24], F32)
        biasp = A("biasp", [128, 1024], BF16)
        biasf = A("biasf", [128, 1024], BF16)
        dwcw = A("dwcw", [64, 48], F32)
        dwcb = A("dwcb", [64, 16], F32)
        xtile = A("xtile", [128, DM], F32)
        sq = A("sq", [128, DM], F32)
        stat = A("stat", [128, 12], F32)
        xnt = A("xnt", [128, DM], BF16)
        xnT = A("xnT", [128, 8 * 512], BF16)
        qkvtile = A("qkvtile", [128, 512], BF16)
        kT = A("kT", [64, S], BF16)
        vT = A("vT", [64, S], BF16)
        qTh = A("qTh", [64, OWN], BF16)
        kdup = A("kdup", [128, 2048], F32)
        qdup = A("qdup", [128, 1024], BF16)
        vtok = A("vtok", [128, NKG * 65], BF16)
        cv1 = A("cv1", [64, 1024], F32)
        cv2 = A("cv2", [64, 1024], F32)
        convb = A("convb", [64, OWN], BF16)
        ltok = A("ltok", [128, NQT * 64], BF16)
        tbuf = A("tbuf", [128, RING * 1024], BF16)
        kern = A("kern", [128, 2 * 1024], BF16)
        usb = A("usb", [65, 512], BF16)
        utok = A("utok", [128, 4 * 65], BF16)
        den = A("den", [128, 4], F32)
        vpair = A("vpair", [64, 128], BF16)
        onec = A("onec", [64, 2], BF16)
        kstall = A("kstall", [2, 32 * 128], BF16)
        qs2 = A("qs2", [2, OWN], BF16)
        ytok = A("ytok", [128, NQT * DM], BF16)
        yTt = A("yTt", [128, 8 * 128], BF16)
        ptile = A("ptile", [128, DM], F32)
        ztile = A("ztile", [128, DM], F32)
        xft = A("xft", [128, DM], BF16)
        mcol = A("mcol", [128, 4], F32)
        scal = A("scal", [1, 32], F32)
        scol = A("scol", [128, 4], F32)
        qint = A("qint", [128, DM], BF16)
        qiT = A("qiT", [128, 8 * 128], BF16)
        ftile = A("ftile", [128, DM], F32)

        psD0 = stk.enter_context(nc.psum_tensor("psD0", [128, 1024], F32))
        psD1 = stk.enter_context(nc.psum_tensor("psD1", [128, 1024], F32))
        psU0 = stk.enter_context(nc.psum_tensor("psU0", [128, 512], F32))
        psU1 = stk.enter_context(nc.psum_tensor("psU1", [128, 512], F32))
        psT = stk.enter_context(nc.psum_tensor("psT", [128, 512], BF16))
        psK = stk.enter_context(nc.psum_tensor("psK", [128, 512], F32))
        psD = [psD0, psD1]
        psU = [psU0, psU1]

        sems, cnt = {}, {}
        for s in ["sd", "sv", "ss", "stt", "sg"]:
            sems[s] = stk.enter_context(nc.semaphore(s))
            cnt[s] = 0

        blk = stk.enter_context(nc.Block())

        def I(instr, s, n):
            instr.then_inc(sems[s], n)
            cnt[s] += n
            return cnt[s]

        def W(eng, s, val=None):
            eng.wait_ge(sems[s], cnt[s] if val is None else val)

        SY, V, SC, T, G = nc.sync, nc.vector, nc.scalar, nc.tensor, nc.gpsimd

        def ln_tile(src, dst_bf16):
            W(V, "sv")
            W(SC, "sv")
            I(V.tensor_reduce(out=stat[:, 0:1], in_=src, axis=AxX, op=Alu.add), "sv", 1)
            W(SC, "sv")
            I(SC.activation(out=sq[:], in_=src, func=Act.Square, accum_out=stat[:, 1:2]), "ss", 1)
            W(V, "ss")
            I(V.tensor_scalar(out=stat[:, 2:3], in0=stat[:, 0:1], scalar1=1.0 / DM,
                              scalar2=None, op0=Alu.mult), "sv", 1)
            W(V, "sv")
            I(V.tensor_tensor(out=stat[:, 3:4], in0=stat[:, 2:3], in1=stat[:, 2:3], op=Alu.mult), "sv", 1)
            I(V.tensor_scalar(out=stat[:, 4:5], in0=stat[:, 1:2], scalar1=1.0 / DM,
                              scalar2=None, op0=Alu.mult), "sv", 1)
            W(V, "sv")
            I(V.tensor_tensor(out=stat[:, 5:6], in0=stat[:, 4:5], in1=stat[:, 3:4], op=Alu.subtract), "sv", 1)
            W(SC, "sv")
            I(SC.activation(out=stat[:, 6:7], in_=stat[:, 5:6], func=Act.Sqrt, bias=stat[:, 10:11], scale=1.0), "ss", 1)
            W(V, "ss")
            I(V.reciprocal(out=stat[:, 7:8], in_=stat[:, 6:7]), "sv", 1)
            W(V, "sv")
            I(V.tensor_scalar(out=dst_bf16, in0=src, scalar1=stat[:, 2:3], scalar2=stat[:, 7:8],
                              op0=Alu.subtract, op1=Alu.mult), "sv", 1)

        def transpose_block(src_ap, dst_ap, n_ident, width=128):
            W(T, "sv")
            I(T.transpose(psT[0:width, 0:n_ident], src_ap, ident[0:n_ident, 0:n_ident]), "stt", 1)
            W(V, "stt")
            I(V.tensor_copy(out=dst_ap, in_=psT[0:width, 0:n_ident]), "sv", 1)

        @blk.sync
        def _(_sy):
            I(V.memset(stat[:, 10:11], 1e-5), "sv", 1)
            I(V.memset(onec[:, 0:1], 1.0), "sv", 1)
            I(V.memset(onec[:, 1:2], -1.0), "sv", 1)
            W(SY, "sd")
            I(SY.dma_start(out=ident[:], in_=identin[:]), "sd", 16)
            W(SY, "sd")
            I(SY.dma_start(out=perm[:, 0:128], in_=permin[0:64, :]), "sd", 16)
            W(SY, "sd")
            I(SY.dma_start(out=perm[:, 128:256], in_=permin[64:128, :]), "sd", 16)
            W(SY, "sd")
            I(SY.dma_start(out=hmask[:], in_=bass.AP(hmaskin.tensor, 0, [[0, 64], [1, 2]])), "sd", 16)
            for i in range(64):
                W(SY, "sd")
                I(SY.dma_start(out=sel[:, i * 128:(i + 1) * 128],
                               in_=selin[i * 128:(i + 1) * 128, :]), "sd", 16)
            for oc in range(24):
                W(SY, "sd")
                I(SY.dma_start(out=biasq[:, oc:oc + 1],
                               in_=bass.AP(qkv_b.tensor, oc * 128, [[1, 128], [1, 1]])), "sd", 16)
            W(SY, "sd")
            I(SY.dma_start(out=wf32[:], in_=bass.AP(proj_b.tensor, 0, [[0, 128], [1, 1024]])), "sd", 16)
            W(V, "sd")
            I(V.tensor_copy(out=biasp[:], in_=wf32[:]), "sv", 1)
            W(SY, "sv")
            W(SY, "sd")
            I(SY.dma_start(out=wf32[:], in_=bass.AP(ffn_b.tensor, 0, [[0, 128], [1, 1024]])), "sd", 16)
            W(V, "sd")
            I(V.tensor_copy(out=biasf[:], in_=wf32[:]), "sv", 1)
            for h in range(16):
                W(SY, "sd")
                I(SY.dma_start(out=dwcw[:, h * 3:h * 3 + 3],
                               in_=bass.AP(dwc_w.tensor, h * 192, [[3, 64], [1, 3]])), "sd", 16)
                W(SY, "sd")
                I(SY.dma_start(out=dwcb[:, h:h + 1],
                               in_=bass.AP(dwc_b.tensor, h * 64, [[1, 64], [1, 1]])), "sd", 16)

            for oc in range(24):
                W(SY, "sv")
                W(SY, "sd")
                I(SY.dma_start(out=wf32[:], in_=qkv_w[oc * 128:(oc + 1) * 128, :]), "sd", 16)
                W(V, "sd")
                W(V, "sd")
                I(V.tensor_copy(out=qwtile[:], in_=wf32[:]), "sv", 1)
                for dmc in range(8):
                    transpose_block(qwtile[:, dmc * 128:(dmc + 1) * 128],
                                    wqTb[:, dmc * 128:(dmc + 1) * 128], 128)
                W(SY, "sv")
                W(SY, "sd")
                I(SY.dma_start(out=bass.AP(wq2.tensor, oc * 128,
                                           [[3 * DM, 128], [128 * 3 * DM, 8], [1, 128]]),
                               in_=wqTb[:, 0:1024]), "sd", 16)
            for oc in range(8):
                W(SY, "sv")
                W(SY, "sd")
                I(SY.dma_start(out=wf32[:], in_=proj_w[oc * 128:(oc + 1) * 128, :]), "sd", 16)
                W(V, "sd")
                I(V.tensor_copy(out=qwtile[:], in_=wf32[:]), "sv", 1)
                for dmc in range(8):
                    transpose_block(qwtile[:, dmc * 128:(dmc + 1) * 128],
                                    wpT[:, dmc * 1024 + oc * 128: dmc * 1024 + (oc + 1) * 128], 128)

            wq_loaded = [None, None]
            wq_used = [None, None]
            for ch in range(8):
                for tt in range(4):
                    t0 = ch * 4 + tt
                    W(SY, "sv")
                    W(SY, "sd")
                    I(SY.dma_start(out=xtile[:], in_=xb[t0 * 128:(t0 + 1) * 128, :]), "sd", 16)
                    W(V, "sd")
                    W(SC, "sd")
                    ln_tile(xtile[:], xnt[:])
                    for dmc in range(8):
                        transpose_block(xnt[:, dmc * 128:(dmc + 1) * 128],
                                        xnT[:, dmc * 512 + tt * 128: dmc * 512 + (tt + 1) * 128], 128)
                ocs = list(range(8, 24)) + (list(range(0, 8)) if ch < 4 else [])
                for j_oc, oc in enumerate(ocs):
                    par = j_oc % 2
                    if j_oc >= 2:
                        W(SY, "stt", wq_used[par])
                    W(SY, "sd")
                    I(SY.dma_start(out=wqTb[:, par * 1024:(par + 1) * 1024],
                                   in_=bass.AP(wq2.tensor, oc * 128,
                                               [[3 * DM, 128], [128 * 3 * DM, 8], [1, 128]])), "sd", 16)
                    wq_loaded[par] = cnt["sd"]
                    W(T, "sv")
                    W(T, "ss")
                    W(T, "sd", wq_loaded[par])
                    mm = None
                    for dmc in range(8):
                        mm = T.matmul(out=psK[:, 0:512],
                                      lhsT=wqTb[:, par * 1024 + dmc * 128: par * 1024 + (dmc + 1) * 128],
                                      rhs=xnT[:, dmc * 512:(dmc + 1) * 512],
                                      start=(dmc == 0), stop=(dmc == 7))
                    wq_used[par] = I(mm, "stt", 1)
                    W(V, "stt")
                    W(V, "sd")
                    I(V.tensor_scalar(out=qkvtile[:], in0=psK[:, 0:512], scalar1=biasq[:, oc:oc + 1],
                                      scalar2=None, op0=Alu.add), "sv", 1)
                    W(SY, "sv")
                    W(SY, "sd")
                    I(SY.dma_start(out=qkvs[oc * 128:(oc + 1) * 128, ch * 512:(ch + 1) * 512],
                                   in_=qkvtile[:]), "sd", 16)

            for h in range(H):
                W(SY, "sv")
                W(SY, "stt")
                W(SY, "sd")
                I(SY.dma_start(out=kT[:], in_=qkvs[1024 + h * 64:1024 + (h + 1) * 64, :]), "sd", 16)
                W(SY, "sd")
                I(SY.dma_start(out=vT[:], in_=qkvs[2048 + h * 64:2048 + (h + 1) * 64, :]), "sd", 16)
                W(SY, "sd")
                I(SY.dma_start(out=qTh[:], in_=qkvs[h * 64:(h + 1) * 64, 0:OWN]), "sd", 16)

                W(V, "sd")
                for kg in range(NKG):
                    W(V, "stt")
                    I(V.tensor_copy(out=vpair[:, 0:64], in_=vT[:, kg * 64:(kg + 1) * 64]), "sv", 1)
                    I(V.tensor_copy(out=vpair[:, 64:128], in_=vT[:, 2048 + kg * 64:2048 + (kg + 1) * 64]), "sv", 1)
                    W(T, "sv")
                    W(T, "sd")
                    I(T.transpose(psT[:, 0:64], vpair[:], ident[0:64, 0:64]), "stt", 1)
                    W(V, "stt")
                    I(V.tensor_copy(out=vtok[:, kg * 65:kg * 65 + 64], in_=psT[:, 0:64]), "sv", 1)
                I(V.memset(bass.AP(vtok, 64, [[NKG * 65, 128], [65, NKG]]), 1.0), "sv", 1)

                w0 = dwcw[:, h * 3 + 0:h * 3 + 1]
                w1 = dwcw[:, h * 3 + 1:h * 3 + 2]
                w2 = dwcw[:, h * 3 + 2:h * 3 + 3]
                I(V.tensor_tensor(out=stat[0:64, 8:9], in0=w0, in1=hmask[:, 0:1], op=Alu.mult), "sv", 1)
                I(V.tensor_tensor(out=stat[0:64, 9:10], in0=w2, in1=hmask[:, 1:2], op=Alu.mult), "sv", 1)
                for hf in range(2):
                    c0 = hf * 1024
                    W(V, "sv")
                    I(V.tensor_scalar(out=cv1[:], in0=vT[:, c0:c0 + 1024], scalar1=w1, scalar2=None,
                                      op0=Alu.mult), "sv", 1)
                    W(V, "sv")
                    I(V.scalar_tensor_tensor(out=cv2[:, 1:1024], in0=vT[:, c0:c0 + 1023], scalar=w0,
                                             in1=cv1[:, 1:1024], op0=Alu.mult, op1=Alu.add), "sv", 1)
                    if hf == 0:
                        I(V.scalar_tensor_tensor(out=cv2[:, 0:1], in0=vT[:, S - 1:S],
                                                 scalar=stat[0:64, 8:9],
                                                 in1=cv1[:, 0:1], op0=Alu.mult, op1=Alu.add), "sv", 1)
                    else:
                        I(V.scalar_tensor_tensor(out=cv2[:, 0:1], in0=vT[:, c0 - 1:c0], scalar=w0,
                                                 in1=cv1[:, 0:1], op0=Alu.mult, op1=Alu.add), "sv", 1)
                    W(V, "sv")
                    I(V.scalar_tensor_tensor(out=cv1[:, 0:1023], in0=vT[:, c0 + 1:c0 + 1024], scalar=w2,
                                             in1=cv2[:, 0:1023], op0=Alu.mult, op1=Alu.add), "sv", 1)
                    if hf == 1:
                        I(V.scalar_tensor_tensor(out=cv1[:, 1023:1024], in0=vT[:, OWN:OWN + 1],
                                                 scalar=stat[0:64, 9:10],
                                                 in1=cv2[:, 1023:1024], op0=Alu.mult, op1=Alu.add), "sv", 1)
                    else:
                        I(V.scalar_tensor_tensor(out=cv1[:, 1023:1024], in0=vT[:, c0 + 1024:c0 + 1025],
                                                 scalar=w2,
                                                 in1=cv2[:, 1023:1024], op0=Alu.mult, op1=Alu.add), "sv", 1)
                    W(V, "sv")
                    I(V.tensor_scalar(out=convb[:, c0:c0 + 1024], in0=cv1[:], scalar1=dwcb[:, h:h + 1],
                                      scalar2=None, op0=Alu.add), "sv", 1)
                for j in range(NQT):
                    transpose_block(convb[:, j * 128:(j + 1) * 128], ltok[:, j * 64:(j + 1) * 64],
                                    64, width=128)

                for chk in range(8):
                    W(T, "sv"); W(T, "ss")
                    mm = T.matmul(out=psK[0:1, 0:512], lhsT=onec[:, 1:2],
                                  rhs=kT[:, chk * 512:(chk + 1) * 512], start=True, stop=True)
                    I(mm, "stt", 1)
                    W(SC, "stt")
                    W(SC, "sd")
                    I(SC.activation(out=usb[0:1, 0:512], in_=psK[0:1, 0:512], func=Act.Copy), "ss", 1)
                    W(SY, "ss")
                    W(SY, "sd")
                    I(SY.dma_start(out=ksd[chk * 512:(chk + 1) * 512], in_=usb[0:1, 0:512]), "sd", 16)
                for chk in range(4):
                    W(T, "sv"); W(T, "ss")
                    mm = T.matmul(out=psK[0:1, 0:512], lhsT=onec[:, 0:1],
                                  rhs=qTh[:, chk * 512:(chk + 1) * 512], start=True, stop=True)
                    I(mm, "stt", 1)
                    W(SC, "stt")
                    W(SC, "sd")
                    I(SC.activation(out=usb[0:1, 0:512], in_=psK[0:1, 0:512], func=Act.Copy), "ss", 1)
                    W(SY, "ss")
                    W(SY, "sd")
                    I(SY.dma_start(out=qsd[chk * 512:(chk + 1) * 512], in_=usb[0:1, 0:512]), "sd", 16)
                W(V, "stt")
                I(V.memset(qs2[0:2, :], 1.0), "sv", 1)
                I(V.memset(kstall[0:2, :], 1.0), "sv", 1)
                W(SY, "sv"); W(SY, "sd")
                W(SY, "sd")
                I(SY.dma_start(out=qs2[1:2, :], in_=bass.AP(qsd.tensor, 0, [[0, 1], [1, OWN]])), "sd", 16)
                W(SY, "stt")
                for kg2 in range(NKG):
                    W(SY, "sd")
                    I(SY.dma_start(out=kstall[0:1, kg2 * 128:(kg2 + 1) * 128],
                                   in_=bass.AP(ksd.tensor, kg2 * 64,
                                               [[0, 1], [2048, 2], [1, 64]])), "sd", 16)
                kst_ready = cnt["sd"]

                for kg in range(NKG):
                    W(T, "sv")
                    W(T, "ss")
                    T.matmul(out=psK[:, 0:64], lhsT=perm[:, 0:128],
                             rhs=kT[:, kg * 64:(kg + 1) * 64], start=True, stop=False)
                    mm = T.matmul(out=psK[:, 0:64], lhsT=perm[:, 128:256],
                                  rhs=kT[:, 2048 + kg * 64:2048 + (kg + 1) * 64],
                                  start=False, stop=True)
                    I(mm, "stt", 1)
                    W(V, "stt")
                    I(V.tensor_copy(out=kdup[:, kg * 64:(kg + 1) * 64], in_=psK[:, 0:64]), "sv", 1)

                for qh2 in range(2):
                    for qc in range(2):
                        W(T, "sv")
                        W(T, "ss")
                        T.matmul(out=psK[:, 0:512], lhsT=perm[:, 0:128],
                                 rhs=qTh[:, qh2 * 1024 + qc * 512: qh2 * 1024 + (qc + 1) * 512],
                                 start=True, stop=False)
                        mm = T.matmul(out=psK[:, 0:512], lhsT=perm[:, 128:256],
                                      rhs=qTh[:, qh2 * 1024 + qc * 512: qh2 * 1024 + (qc + 1) * 512],
                                      start=False, stop=True)
                        I(mm, "stt", 1)
                        W(V, "stt")
                        I(V.tensor_copy(out=qdup[:, qc * 512:(qc + 1) * 512], in_=psK[:, 0:512]), "sv", 1)

                    slot_free = [None] * RING
                    prod_cnt = [None] * RING
                    exp_hist = {}
                    umm_hist = {}
                    for kg in range(NKG):
                        for i in range(64):
                            gidx = kg * 64 + i
                            slot = gidx % RING
                            if gidx == 0:
                                W(V, "sv")
                            if gidx % GRP == 0:
                                tgt = slot_free[(gidx + GRP - 1) % RING]
                                if tgt is not None:
                                    W(V, "stt", tgt)
                            ts = V.tensor_scalar(out=tbuf[:, slot * 1024:(slot + 1) * 1024],
                                                 in0=qdup[:], scalar1=kdup[:, gidx:gidx + 1],
                                                 scalar2=0.0, op0=Alu.subtract, op1=Alu.min)
                            prod_cnt[slot] = I(ts, "sv", 1)
                            if gidx % GRP == 0:
                                W(T, "sv", prod_cnt[slot] + GRP - 1)
                            if i == 0:
                                if kg >= 2:
                                    W(T, "ss", exp_hist[kg - 2])
                                elif kg < 2:
                                    W(T, "ss")
                            mm = None
                            for qc in range(2):
                                mm = T.matmul(out=psD[kg % 2][:, qc * 512:(qc + 1) * 512],
                                              lhsT=sel[:, i * 128:(i + 1) * 128],
                                              rhs=tbuf[:, slot * 1024 + qc * 512: slot * 1024 + (qc + 1) * 512],
                                              start=(i == 0), stop=False)
                            slot_free[slot] = I(mm, "stt", 1)
                        if kg == 0 and qh2 == 0:
                            W(T, "sd", kst_ready)
                        mm = None
                        for qc in range(2):
                            mm = T.matmul(out=psD[kg % 2][:, qc * 512:(qc + 1) * 512],
                                          lhsT=kstall[:, kg * 128:(kg + 1) * 128],
                                          rhs=qs2[:, qh2 * 1024 + qc * 512: qh2 * 1024 + (qc + 1) * 512],
                                          start=False, stop=True)
                        I(mm, "stt", 1)
                        W(SC, "stt")
                        if kg >= 2:
                            W(SC, "stt", umm_hist[kg - 2])
                        a = SC.activation(out=kern[:, (kg % 2) * 1024:(kg % 2 + 1) * 1024],
                                          in_=psD[kg % 2][:], func=Act.Exp, scale=-0.0625)
                        exp_hist[kg] = I(a, "ss", 1)
                        if kg >= 1:
                            pk = (kg - 1) % 2
                            if kg == 1:
                                W(T, "ss")
                            else:
                                W(T, "ss", exp_hist[kg - 1])
                            mm = None
                            for qc in range(2):
                                mm = T.matmul(out=psU[qc][0:65, :],
                                              lhsT=vtok[:, (kg - 1) * 65:(kg - 1) * 65 + 65],
                                              rhs=kern[:, pk * 1024 + qc * 512: pk * 1024 + (qc + 1) * 512],
                                              start=(kg - 1 == 0), stop=False)
                            umm_hist[kg - 1] = I(mm, "stt", 1)
                    W(T, "ss", exp_hist[31])
                    mm = None
                    for qc in range(2):
                        mm = T.matmul(out=psU[qc][0:65, :],
                                      lhsT=vtok[:, 31 * 65:31 * 65 + 65],
                                      rhs=kern[:, (31 % 2) * 1024 + qc * 512: (31 % 2) * 1024 + (qc + 1) * 512],
                                      start=False, stop=True)
                    umm_hist[31] = I(mm, "stt", 1)

                    for qc in range(2):
                        W(SC, "stt")
                        I(SC.activation(out=usb[:], in_=psU[qc][0:65, :], func=Act.Copy), "ss", 1)
                        for j in range(4):
                            jg = qh2 * 8 + qc * 4 + j
                            W(T, "ss")
                            W(T, "sv")
                            I(T.transpose(psT[:, 0:65], usb[:, j * 128:(j + 1) * 128],
                                          ident[0:65, 0:65]), "stt", 1)
                            W(V, "stt")
                            I(V.tensor_copy(out=utok[:, j * 65:(j + 1) * 65],
                                            in_=psT[:, 0:65]), "sv", 1)
                            W(V, "sv")
                            I(V.tensor_scalar(out=den[:, j:j + 1],
                                              in0=utok[:, j * 65 + 64:j * 65 + 65],
                                              scalar1=1e-6, scalar2=None, op0=Alu.add), "sv", 1)
                            W(V, "sv")
                            I(V.reciprocal(out=den[:, j:j + 1], in_=den[:, j:j + 1]), "sv", 1)
                            W(V, "sv")
                            I(V.scalar_tensor_tensor(
                                out=ytok[:, jg * DM + h * 64: jg * DM + (h + 1) * 64],
                                in0=utok[:, j * 65:j * 65 + 64], scalar=den[:, j:j + 1],
                                in1=ltok[:, jg * 64:(jg + 1) * 64],
                                op0=Alu.mult, op1=Alu.add), "sv", 1)

            I(V.memset(mcol[:, 0:1], 0.0), "sv", 1)
            for tt in range(NQT):
                for cc in range(8):
                    transpose_block(ytok[:, tt * DM + cc * 128: tt * DM + (cc + 1) * 128],
                                    yTt[:, cc * 128:(cc + 1) * 128], 128)
                for o2c in range(2):
                    W(T, "sv")
                    mm = None
                    for cc in range(8):
                        mm = T.matmul(out=psD[o2c][:, 0:512],
                                      lhsT=yTt[:, cc * 128:(cc + 1) * 128],
                                      rhs=wpT[:, cc * 1024 + o2c * 512: cc * 1024 + (o2c + 1) * 512],
                                      start=(cc == 0), stop=(cc == 7))
                    I(mm, "stt", 1)
                W(V, "stt")
                for o2c in range(2):
                    I(V.tensor_tensor(out=ptile[:, o2c * 512:(o2c + 1) * 512],
                                      in0=psD[o2c][:, 0:512],
                                      in1=biasp[:, o2c * 512:(o2c + 1) * 512], op=Alu.add), "sv", 1)
                ln_tile(ptile[:], xnt[:])
                W(SY, "sv")
                W(SY, "sd")
                I(SY.dma_start(out=xtile[:], in_=xb[tt * 128:(tt + 1) * 128, :]), "sd", 16)
                W(V, "sd")
                I(V.tensor_tensor(out=ztile[:], in0=xtile[:], in1=xnt[:], op=Alu.add), "sv", 1)
                W(SY, "sv")
                W(SY, "sd")
                I(SY.dma_start(out=zs[tt * 128:(tt + 1) * 128, :], in_=ztile[:]), "sd", 16)
                ln_tile(ztile[:], xft[:])
                W(V, "sv")
                I(V.tensor_reduce(out=mcol[:, 1:2], in_=xft[:], axis=AxX,
                                  op=Alu.max, apply_absolute_value=True), "sv", 1)
                W(V, "sv")
                I(V.tensor_tensor(out=mcol[:, 0:1], in0=mcol[:, 0:1], in1=mcol[:, 1:2],
                                  op=Alu.max), "sv", 1)
                W(SY, "sv")
                W(SY, "sd")
                I(SY.dma_start(out=xfs[tt * 128:(tt + 1) * 128, :], in_=xft[:]), "sd", 16)

            I(V.memset(mcol[:, 2:3], 0.0), "sv", 1)
            for oc in range(8):
                W(SY, "sv")
                W(SY, "sd")
                I(SY.dma_start(out=wf32[:], in_=ffn_w[oc * 128:(oc + 1) * 128, :]), "sd", 16)
                W(V, "sd")
                I(V.tensor_reduce(out=mcol[:, 3:4], in_=wf32[:], axis=AxX, op=Alu.add,
                                  apply_absolute_value=True), "sv", 1)
                W(V, "sv")
                I(V.tensor_tensor(out=mcol[:, 2:3], in0=mcol[:, 2:3], in1=mcol[:, 3:4],
                                  op=Alu.add), "sv", 1)
            W(V, "sv")
            I(V.tensor_copy(out=qwtile[:, 0:1], in_=mcol[:, 0:1]), "sv", 1)
            I(V.tensor_copy(out=qwtile[:, 1:2], in_=mcol[:, 2:3]), "sv", 1)
            transpose_block(qwtile[:, 0:1], sq[0:1, 0:128], 128, width=1)
            transpose_block(qwtile[:, 1:2], sq[0:1, 128:256], 128, width=1)
            W(V, "sv")
            I(V.tensor_reduce(out=scal[0:1, 0:1], in_=sq[0:1, 0:128], axis=AxX, op=Alu.max), "sv", 1)
            I(V.tensor_reduce(out=scal[0:1, 1:2], in_=sq[0:1, 128:256], axis=AxX, op=Alu.add), "sv", 1)
            W(V, "sv")
            I(V.tensor_scalar(out=scal[0:1, 1:2], in0=scal[0:1, 1:2], scalar1=float(2.0 ** -20),
                              scalar2=None, op0=Alu.mult), "sv", 1)
            W(V, "sv")
            I(V.tensor_scalar(out=scal[0:1, 2:3], in0=scal[0:1, 0:1], scalar1=float(2.0 ** 18),
                              scalar2=None, op0=Alu.mult), "sv", 1)
            W(V, "sv")
            I(V.tensor_copy(out=ftile[0:1, 0:1].bitcast(I32), in_=scal[0:1, 2:3]), "sv", 1)
            W(V, "sv")
            I(V.tensor_copy(out=scal[0:1, 2:3], in_=ftile[0:1, 0:1].bitcast(I32)), "sv", 1)
            W(V, "sv")
            I(V.scalar_tensor_tensor(out=scal[0:1, 3:4], in0=scal[0:1, 2:3],
                                     scalar=float(-(2.0 ** -18)), in1=scal[0:1, 0:1],
                                     op0=Alu.mult, op1=Alu.add), "sv", 1)
            W(V, "sv")
            I(V.tensor_scalar(out=scal[0:1, 4:5], in0=scal[0:1, 3:4], scalar1=float(2.0 ** 42),
                              scalar2=None, op0=Alu.mult), "sv", 1)
            W(V, "sv")
            I(V.tensor_copy(out=ftile[0:1, 1:2].bitcast(I32), in_=scal[0:1, 4:5]), "sv", 1)
            W(V, "sv")
            I(V.tensor_copy(out=scal[0:1, 4:5], in_=ftile[0:1, 1:2].bitcast(I32)), "sv", 1)
            W(V, "sv")
            I(V.tensor_copy(out=scal[0:1, 5:6], in_=scal[0:1, 2:3]), "sv", 1)
            I(V.tensor_copy(out=scal[0:1, 6:7], in_=scal[0:1, 4:5]), "sv", 1)
            W(SC, "sv")
            W(SC, "ss")
            I(SC.dma_start(out=cc_in[:], in_=scal[0:1, 5:7]), "ss", 16)
            W(G, "ss")
            I(G.collective_compute("AllGather", Alu.bypass, ins=[cc_in[:]], outs=[cc_out[:]],
                                   replica_groups=[[0, 1, 2, 3, 4, 5, 6, 7]]), "sg", 1)
            W(SC, "sg")
            W(SC, "ss")
            I(SC.dma_start(out=scal[0:1, 8:24],
                           in_=bass.AP(cc_out.tensor, 0, [[16, 1], [1, 16]])), "ss", 16)
            W(V, "ss")
            h_view = bass.AP(scal, 8, [[32, 1], [2, 8]])
            l_view = bass.AP(scal, 9, [[32, 1], [2, 8]])
            I(V.tensor_scalar(out=scal[0:1, 24:32], in0=h_view, scalar1=float(2.0 ** -18),
                              scalar2=None, op0=Alu.mult), "sv", 1)
            W(V, "sv")
            I(V.scalar_tensor_tensor(out=scal[0:1, 8:16], in0=l_view, scalar=float(2.0 ** -42),
                                     in1=scal[0:1, 24:32], op0=Alu.mult, op1=Alu.add), "sv", 1)
            W(V, "sv")
            I(V.tensor_reduce(out=scal[0:1, 16:17], in_=scal[0:1, 8:16], axis=AxX, op=Alu.max), "sv", 1)
            W(V, "sv")
            I(V.reciprocal(out=scal[0:1, 17:18], in_=scal[0:1, 16:17]), "sv", 1)
            W(V, "sv")
            I(V.tensor_scalar(out=scal[0:1, 18:19], in0=scal[0:1, 17:18], scalar1=127.0,
                              scalar2=None, op0=Alu.mult), "sv", 1)
            I(V.tensor_scalar(out=scal[0:1, 19:20], in0=scal[0:1, 16:17],
                              scalar1=float(1.0 / 127.0), scalar2=None, op0=Alu.mult), "sv", 1)
            W(V, "sv")
            I(V.tensor_tensor(out=scal[0:1, 20:21], in0=scal[0:1, 19:20], in1=scal[0:1, 1:2],
                              op=Alu.mult), "sv", 1)
            I(V.tensor_scalar(out=scal[0:1, 21:22], in0=scal[0:1, 1:2], scalar1=0.5,
                              scalar2=None, op0=Alu.mult), "sv", 1)
            W(SY, "sv")
            W(SY, "sd")
            I(SY.dma_start(out=sc_d[0:4], in_=scal[0:1, 18:22]), "sd", 16)
            W(SY, "sd")
            W(SY, "sd")
            I(SY.dma_start(out=scol[:, 0:4],
                           in_=bass.AP(sc_d.tensor, 0, [[0, 128], [1, 4]])), "sd", 16)

            for oc in range(8):
                W(SY, "sv")
                W(SY, "ss")
                W(SY, "sd")
                I(SY.dma_start(out=wf32[:], in_=ffn_w[oc * 128:(oc + 1) * 128, :]), "sd", 16)
                W(SC, "sd")
                I(SC.activation(out=ptile[:], in_=wf32[:], func=Act.Abs), "ss", 1)
                W(SC, "ss")
                I(SC.activation(out=ztile[:], in_=wf32[:], func=Act.Sign), "ss", 1)
                W(V, "ss")
                W(V, "sv")
                I(V.tensor_scalar(out=ftile[:], in0=ptile[:], scalar1=scol[:, 3:4], scalar2=None,
                                  op0=Alu.is_gt), "sv", 1)
                W(V, "sv")
                W(V, "ss")
                I(V.tensor_tensor(out=qwtile[:], in0=ftile[:], in1=ztile[:], op=Alu.mult), "sv", 1)
                for kc in range(8):
                    transpose_block(qwtile[:, kc * 128:(kc + 1) * 128],
                                    yTt[:, kc * 128:(kc + 1) * 128], 128)
                W(SY, "sv")
                W(SY, "sd")
                I(SY.dma_start(out=bass.AP(qw2.tensor, oc * 128,
                                           [[DM, 128], [128 * DM, 8], [1, 128]]),
                               in_=yTt[:, 0:1024]), "sd", 16)

            qw_loaded = [None] * 4
            qw_used = [None] * 4
            for tt in range(NQT):
                W(SY, "sv")
                W(SY, "sd")
                I(SY.dma_start(out=xft[:], in_=xfs[tt * 128:(tt + 1) * 128, :]), "sd", 16)
                W(V, "stt")
                W(V, "sd")
                I(V.tensor_scalar(out=ftile[:], in0=xft[:],
                                  scalar1=scol[:, 0:1], scalar2=None, op0=Alu.mult), "sv", 1)
                W(SC, "sv")
                I(SC.activation(out=sq[:], in_=ftile[:], func=Act.Sign), "ss", 1)
                W(V, "ss")
                I(V.scalar_tensor_tensor(out=ptile[:], in0=sq[:], scalar=0.5, in1=ftile[:],
                                         op0=Alu.mult, op1=Alu.add), "sv", 1)
                W(V, "sv")
                I(V.tensor_copy(out=ftile[:].bitcast(I32), in_=ptile[:]), "sv", 1)
                W(V, "sv")
                I(V.tensor_copy(out=qint[:], in_=ftile[:].bitcast(I32)), "sv", 1)
                for kc in range(8):
                    transpose_block(qint[:, kc * 128:(kc + 1) * 128],
                                    qiT[:, kc * 128:(kc + 1) * 128], 128)
                for oc2 in range(2):
                    for kc in range(8):
                        slot = kc % 4
                        if qw_used[slot] is not None:
                            W(SY, "stt", qw_used[slot])
                        W(SY, "sd")
                        I(SY.dma_start(out=wqTb[:, slot * 512:(slot + 1) * 512],
                                       in_=bass.AP(qw2.tensor, kc * 128 * DM + oc2 * 512,
                                                   [[DM, 128], [1, 512]])), "sd", 16)
                        qw_loaded[slot] = cnt["sd"]
                        if kc == 0:
                            W(T, "sv")
                        W(T, "sd", qw_loaded[slot])
                        mm = T.matmul(out=psD[oc2][:, 0:512],
                                      lhsT=qiT[:, kc * 128:(kc + 1) * 128],
                                      rhs=wqTb[:, slot * 512:(slot + 1) * 512],
                                      start=(kc == 0), stop=(kc == 7))
                        qw_used[slot] = I(mm, "stt", 1)
                W(V, "stt")
                for oc2 in range(2):
                    I(V.scalar_tensor_tensor(out=ftile[:, oc2 * 512:(oc2 + 1) * 512],
                                             in0=psD[oc2][:, 0:512], scalar=scol[:, 2:3],
                                             in1=biasf[:, oc2 * 512:(oc2 + 1) * 512],
                                             op0=Alu.mult, op1=Alu.add), "sv", 1)
                ln_tile(ftile[:], xnt[:])
                W(SY, "sv")
                W(SY, "sd")
                I(SY.dma_start(out=ztile[:], in_=zs[tt * 128:(tt + 1) * 128, :]), "sd", 16)
                W(V, "sd")
                I(V.tensor_copy(out=ptile[:], in_=xnt[:]), "sv", 1)
                W(V, "sv")
                I(V.tensor_tensor(out=ftile[:], in0=ztile[:], in1=ptile[:], op=Alu.add), "sv", 1)
                W(SY, "sv")
                W(SY, "sd")
                I(SY.dma_start(out=out[tt * 128:(tt + 1) * 128, :], in_=ftile[:]), "sd", 16)

            W(SY, "sd")

    nc.finalize()
    return nc


def _host_constants():
    selv = np.zeros((64 * 128, 128), dtype=np.float32)
    for i in range(64):
        selv[i * 128: i * 128 + 64, i] = -2.0
        selv[i * 128 + 64: i * 128 + 128, 64 + i] = -2.0
    permv = np.zeros((128, 128), dtype=np.float32)
    for d in range(64):
        permv[d, d] = 1.0
        permv[64 + d, 64 + d] = 1.0
    identv = np.eye(128, dtype=np.float32)
    return selv.astype(BB), permv.astype(BB), identv.astype(BB)


_NC_CACHE = None


def kernel(L0_attn_x, L0_attn_qkv_w, L0_attn_qkv_b, L0_attn_proj_w, L0_attn_proj_b,
           L0_attn_dwc_w, L0_attn_dwc_b, L0_ffn_weight, L0_ffn_bias):
    global _NC_CACHE
    if _NC_CACHE is None:
        _NC_CACHE = build_program()
    nc = _NC_CACHE

    x = np.asarray(L0_attn_x, dtype=np.float32)
    selv, permv, identv = _host_constants()
    common = dict(
        qkv_w=np.ascontiguousarray(L0_attn_qkv_w, dtype=np.float32),
        qkv_b=np.ascontiguousarray(L0_attn_qkv_b, dtype=np.float32),
        proj_w=np.ascontiguousarray(L0_attn_proj_w, dtype=np.float32),
        proj_b=np.ascontiguousarray(L0_attn_proj_b, dtype=np.float32),
        dwc_w=np.ascontiguousarray(np.asarray(L0_attn_dwc_w, dtype=np.float32).reshape(DM, 3)),
        dwc_b=np.ascontiguousarray(L0_attn_dwc_b, dtype=np.float32),
        ffn_w=np.ascontiguousarray(L0_ffn_weight, dtype=np.float32),
        ffn_b=np.ascontiguousarray(L0_ffn_bias, dtype=np.float32),
        selin=selv, permin=permv, identin=identv,
    )
    in_maps = []
    for c in range(8):
        b, qh = c // 2, c % 2
        m = dict(common)
        m["xb"] = np.ascontiguousarray(np.roll(x[b], -qh * OWN, axis=0))
        m["hmaskin"] = np.array([1.0 if qh == 1 else 0.0, 1.0 if qh == 0 else 0.0],
                                dtype=np.float32)
        in_maps.append(m)

    res = run_bass_kernel_spmd(nc, in_maps, core_ids=list(range(8)))
    outp = np.empty((B, S, DM), dtype=np.float32)
    for c in range(8):
        b, qh = c // 2, c % 2
        outp[b, qh * OWN:(qh + 1) * OWN] = res.results[c]["out"]
    return outp


if __name__ == "__main__":
    import time
    t0 = time.time()
    prog = build_program()
    n = sum(1 for _ in prog.all_instructions())
    print(f"built in {time.time()-t0:.1f}s, instructions: {n}")
